# version 09
# baseline (speedup 1.0000x reference)
import contextlib
import numpy as np
import ml_dtypes
import concourse.bass as bass
import concourse.mybir as mybir
from concourse.bass_utils import run_bass_kernel_spmd

F32 = mybir.dt.float32
BF16 = mybir.dt.bfloat16
U8 = mybir.dt.uint8
AF = mybir.ActivationFunctionType
ALU = mybir.AluOpType
AX = mybir.AxisListType.X
bfdt = ml_dtypes.bfloat16

C = 2048
NCH = 16
MLP = 16384
SINK = 4
WH = 32
WD = 64
NEG = -30000.0
EXPB = -12.0
LN_EPS = 1e-5
GN_EPS = 1e-5 * 64


class Prog:
    ENG = ["sync", "gpsimd", "vector", "scalar", "tensor"]
    CHAINED = {"vector", "scalar", "gpsimd"}

    def __init__(self, nc):
        self.nc = nc
        self.sem = {}
        self.cnt = {}
        self.seen = {e: {} for e in self.ENG}
        self.ops = {e: [] for e in self.ENG}
        self.lastop = {e: None for e in self.ENG}
        for e in self.ENG:
            self._mk("E" + e)

    def _mk(self, key):
        if key not in self.sem:
            self.sem[key] = self.nc.alloc_semaphore("sm_" + key)
            self.cnt[key] = 0
        return key

    def emit(self, engine, fn, deps=()):
        key = "E" + engine
        waits = {}
        alldeps = list(deps)
        if engine in self.CHAINED and self.lastop[engine] is not None:
            alldeps.append(self.lastop[engine])
        for d in alldeps:
            if d is None:
                continue
            k, n = d
            if n > self.seen[engine].get(k, 0) and n > waits.get(k, 0):
                waits[k] = n
        for k, n in waits.items():
            self.seen[engine][k] = n
        self.cnt[key] += 1
        tok = (key, self.cnt[key])
        self.ops[engine].append((waits, fn, key, 1))
        if engine in self.CHAINED:
            self.lastop[engine] = tok
        return tok

    def dma(self, engine, out, in_, deps=(), stream="misc"):
        key = self._mk("D" + engine[0] + ":" + stream)
        waits = {}
        for d in deps:
            if d is None:
                continue
            k, n = d
            if n > self.seen[engine].get(k, 0) and n > waits.get(k, 0):
                waits[k] = n
        for k, n in waits.items():
            self.seen[engine][k] = n
        self.cnt[key] += 16
        tok = (key, self.cnt[key])
        self.ops[engine].append((waits, lambda e, o=out, i=in_: e.dma_start(out=o, in_=i), key, 16))
        return tok

    def barrier(self):
        toks = [(k, self.cnt[k]) for k in self.cnt]
        outs = []
        for e in self.ENG:
            outs.append(self.emit(e, lambda eng: eng.nop(), deps=toks))
        return outs

    def build(self):
        with self.nc.Block() as block:
            def mk(e):
                def body(eng):
                    for waits, fn, key, inc in self.ops[e]:
                        for k in sorted(waits):
                            eng.wait_ge(self.sem[k], waits[k])
                        fn(eng).then_inc(self.sem[key], inc)
                return body
            block.sync(mk("sync"))
            block.gpsimd(mk("gpsimd"))
            block.vector(mk("vector"))
            block.scalar(mk("scalar"))
            block.tensor(mk("tensor"))


def later(*toks):
    return [t for t in toks if t is not None]


def build_program(T=4096, stop_after='all'):
    nc = bass.Bass("TRN2", target_bir_lowering=False)
    NTB = T // 512
    NCK = T // 128

    def din(name, shape, dt=F32):
        return nc.dram_tensor(name, shape, dt, kind="ExternalInput").ap()

    x_in = din("x", [C, T])
    wqT = din("wqT", [C, C]); wkT = din("wkT", [C, 512]); wvT = din("wvT", [C, 512]); woT = din("woT", [C, C])
    brT = din("brT", [C, C]); bkT = din("bkT", [C, C]); bvT = din("bvT", [C, C])
    bgT = din("bgT", [C, C]); boT = din("boT", [C, C])
    maa = din("maa", [128, 7 * NCH])
    lnwb = din("lnwb", [2, C])
    w1 = din("w1", [C, 160]); w2 = din("w2", [5, 32, C])
    dw1 = din("dw1", [C, 64]); dw2 = din("dw2", [64, C])
    uT = din("uT", [WD, WH])
    gpk = din("gpk", [1024, MLP], U8); gsc = din("gsc", [64, MLP])
    upk = din("upk", [1024, MLP], U8); usc = din("usc", [64, MLP])
    dpk = din("dpk", [8192, C], U8); dsc = din("dsc", [512, C])
    out = nc.dram_tensor("out", [C, T], F32, kind="ExternalOutput").ap()

    def scrt(name, shape, dt):
        return nc.dram_tensor(name, shape, dt, kind="Internal").ap()

    xbb = scrt("xb", [C, T], F32)
    qh = scrt("qh", [NCH, 128, T], BF16)
    kh = scrt("kh", [4, 128, T], BF16)
    vh = scrt("vh", [T, 512], BF16)
    wgq = scrt("wgq", [C, MLP], BF16)
    wuq = scrt("wuq", [C, MLP], BF16)
    wdq = scrt("wdq", [NCH, 128, MLP], BF16)
    inter = scrt("inter", [MLP, T], BF16)
    xnh = scrt("xnh", [C, T], BF16)
    rr_h = scrt("rr_h", [NCK, WD, WH * 128], BF16)
    kk_h = scrt("kk_h", [NCK, WD, WH * 128], BF16)
    ee_h = scrt("ee_h", [NCK, WD, WH * 128], F32)
    vb_h = scrt("vb_h", [T, C], BF16)
    gg_h = scrt("gg_h", [T, C], BF16)
    brB = scrt("brB", [C, C], BF16)
    bkB = scrt("bkB", [C, C], BF16)
    bvB = scrt("bvB", [C, C], BF16)
    bgB = scrt("bgB", [C, C], BF16)
    boB = scrt("boB", [C, C], BF16)

    jj, ii = np.meshgrid(np.arange(128), np.arange(128), indexing="ij")
    ident_c = nc.inline_tensor(np.eye(128, dtype=bfdt), "ident_c").ap()
    m_causal = nc.inline_tensor(np.where(jj <= ii, 0.0, NEG).astype(np.float32), "m_causal").ap()
    m_band = nc.inline_tensor(np.where(jj > ii, 0.0, NEG).astype(np.float32), "m_band").ap()
    m_bandsink = nc.inline_tensor(np.where((jj > ii) | (jj < SINK), 0.0, NEG).astype(np.float32), "m_bandsink").ap()
    m_strict = nc.inline_tensor(np.where(jj < ii, 1.0, 0.0).astype(np.float32), "m_strict").ap()
    m_strict4 = nc.inline_tensor(np.tile(np.where(jj < ii, 1.0, 0.0).astype(np.float32), (1, 4)), "m_strict4").ap()

    P = Prog(nc)
    sb = nc.sbuf_tensor
    ps = nc.psum_tensor
    glob = contextlib.ExitStack()

    ident = glob.enter_context(sb("ident", [128, 128], BF16))
    mc_t = glob.enter_context(sb("mc_t", [128, 128], F32))
    mb_t = glob.enter_context(sb("mb_t", [128, 128], F32))
    mbs_t = glob.enter_context(sb("mbs_t", [128, 128], F32))
    mst_t = glob.enter_context(sb("mst_t", [128, 128], F32))
    mst4_t = glob.enter_context(sb("mst4_t", [128, 512], F32))
    onesc = glob.enter_context(sb("onesc", [128, 1], BF16))
    ones1 = glob.enter_context(sb("ones1", [1, 128], BF16))
    maa_s = glob.enter_context(sb("maa_s", [128, 7 * NCH], F32))
    cst = glob.enter_context(sb("cst", [128, 4], F32))
    tCONST = [
        P.emit("vector", lambda e: e.memset(cst[:, 0:1], 7.5)),
        P.emit("vector", lambda e: e.memset(cst[:, 1:2], EXPB)),
        P.emit("vector", lambda e: e.memset(cst[:, 2:3], -1.0)),
        P.emit("vector", lambda e: e.memset(cst[:, 3:4], 0.08838834764831845)),
        P.dma("sync", ident[:], ident_c, stream="const"),
        P.dma("sync", mc_t[:], m_causal, stream="const"),
        P.dma("sync", mb_t[:], m_band, stream="const"),
        P.dma("sync", mbs_t[:], m_bandsink, stream="const"),
        P.dma("sync", mst_t[:], m_strict, stream="const"),
        P.dma("sync", mst4_t[:], m_strict4, stream="const"),
        P.dma("sync", maa_s[:], maa, stream="const"),
        P.emit("vector", lambda e: e.memset(onesc[:], 1.0)),
        P.emit("vector", lambda e: e.memset(ones1[:], 1.0)),
    ]

    def cast_load(dst_ap, src_ap, deps=(), stream="wcast"):
        return P.dma("gpsimd", dst_ap, src_ap, deps=deps, stream=stream)

    tBW = [P.dma("gpsimd", d, s, stream="wprep") for d, s in
           [(brB, brT), (bkB, bkT), (bvB, bvT), (bgB, bgT), (boB, boT)]]

    def ln_pass(xsrc, tag, pre, consume):
        with contextlib.ExitStack() as ctx:
            xbf = [ctx.enter_context(sb(f"{tag}xb{i}", [128, NCH * 512], BF16)) for i in range(2)]
            sqb = ctx.enter_context(sb(f"{tag}sq", [128, NCH * 512], BF16))
            rowf = ctx.enter_context(sb(f"{tag}rf", [1, 4 * 512], F32))
            rowb = ctx.enter_context(sb(f"{tag}rb", [1, 2 * 512], BF16))
            p_bc = ctx.enter_context(ps(f"{tag}pb", [128, 1024], F32))
            p_st = p_bc[0:1, :]
            t_x_r = [list(pre), list(pre)]
            t_bcfree = None
            t_stfree = None
            for tb in range(NTB):
                t0 = tb * 512
                b = tb % 2
                src = xsrc[:, t0:t0 + 512].rearrange("(kc p) t -> p kc t", p=128)
                tld = P.dma("gpsimd", xbf[b][:].rearrange("p (kc t) -> p kc t", t=512), src,
                            deps=later(*t_x_r[b], *tCONST), stream="lnx" + str(b))
                tsq = P.emit("scalar", lambda e, bb=b: e.activation(sqb[:], xbf[bb][:], AF.Square), deps=[tld])
                for kc in range(NCH):
                    tmm = P.emit("tensor", lambda e, k=kc, bb=b: e.matmul(
                        p_st[:, 0:512], onesc[:], xbf[bb][:, k * 512:(k + 1) * 512],
                        start=(k == 0), stop=(k == NCH - 1)),
                        deps=later(tld, t_stfree, t_bcfree, *tCONST) if kc == 0 else ())
                for kc in range(NCH):
                    tmm = P.emit("tensor", lambda e, k=kc: e.matmul(
                        p_st[:, 512:1024], onesc[:], sqb[:, k * 512:(k + 1) * 512],
                        start=(k == 0), stop=(k == NCH - 1)),
                        deps=[tsq] if kc == 0 else ())
                r1 = P.emit("vector", lambda e: e.tensor_scalar(rowf[:, 0:512], p_st[:, 0:512], -1.0 / C, None, op0=ALU.mult), deps=[tmm])
                P.emit("vector", lambda e: e.tensor_tensor(rowf[:, 512:1024], rowf[:, 0:512], rowf[:, 0:512], op=ALU.mult))
                r3 = P.emit("vector", lambda e: e.scalar_tensor_tensor(rowf[:, 1024:1536], p_st[:, 512:1024], 1.0 / C,
                                                                      rowf[:, 512:1024], op0=ALU.mult, op1=ALU.subtract))
                t_stfree = r3
                P.emit("vector", lambda e: e.tensor_scalar(rowf[:, 1024:1536], rowf[:, 1024:1536], LN_EPS, None, op0=ALU.add))
                r5 = P.emit("vector", lambda e: e.reciprocal(rowf[:, 1536:2048], rowf[:, 1024:1536]))
                r6 = P.emit("scalar", lambda e: e.activation(rowb[:, 512:1024], rowf[:, 1536:2048], AF.Sqrt), deps=[r5])
                r7 = P.emit("vector", lambda e: e.tensor_copy(rowb[:, 0:512], rowf[:, 0:512]))
                b1 = P.emit("tensor", lambda e: e.matmul(p_bc[:, 0:512], ones1[:], rowb[:, 0:512], start=True, stop=True),
                            deps=later(r7, t_bcfree))
                b2 = P.emit("tensor", lambda e: e.matmul(p_bc[:, 512:1024], ones1[:], rowb[:, 512:1024], start=True, stop=True),
                            deps=[r6])
                tlast = None
                for kc in range(NCH):
                    P.emit("vector", lambda e, k=kc, bb=b: e.tensor_tensor(
                        xbf[bb][:, k * 512:(k + 1) * 512], xbf[bb][:, k * 512:(k + 1) * 512], p_bc[:, 0:512], op=ALU.add),
                        deps=later(b1, b2, tsq) if kc == 0 else ())
                    tlast = P.emit("vector", lambda e, k=kc, bb=b: e.tensor_tensor(
                        xbf[bb][:, k * 512:(k + 1) * 512], xbf[bb][:, k * 512:(k + 1) * 512], p_bc[:, 512:1024], op=ALU.mult))
                t_bcfree = tlast
                rd = consume(tb, xbf[b], tlast)
                t_x_r[b] = rd if isinstance(rd, list) else [rd]

    with contextlib.ExitStack() as ctx:
        NS = 4096
        pk = [ctx.enter_context(sb(f"dqpk{i}", [128, NS], U8)) for i in range(2)]
        sg32 = [ctx.enter_context(sb(f"dqsg{i}", [128, NS], F32)) for i in range(2)]
        sgb = ctx.enter_context(sb("dqsgb", [128, NS], BF16))
        c8 = [ctx.enter_context(sb(f"dqc8{i}", [128, NS], U8)) for i in range(2)]
        cbf = [ctx.enter_context(sb(f"dqcbf{i}", [128, NS], BF16)) for i in range(2)]
        sgn = [ctx.enter_context(sb(f"dqsgn{i}", [128, NS], BF16)) for i in range(2)]
        t_cbf_r = [None, None]
        t_sgn_r = [None, None]
        mcb = ctx.enter_context(sb("dqmc", [128, NS], BF16))
        b2b = ctx.enter_context(sb("dqb2", [128, NS], BF16))
        m1b = ctx.enter_context(sb("dqm1", [128, NS], BF16))
        rrb = ctx.enter_context(sb("dqrr", [128, NS], BF16))
        wo2 = [ctx.enter_context(sb(f"dqwo{i}", [128, NS], BF16)) for i in range(4)]

        jobs = []
        for (pkap, scap, kind, nrb, nsl, wid) in [
                (gpk, gsc, "g", 8, 4, NS), (upk, usc, "u", 8, 4, NS), (dpk, dsc, "d", 64, 1, C)]:
            for rb in range(nrb):
                for nsi in range(nsl):
                    jobs.append((pkap, scap, kind, rb, nsi, wid))

        t_pk_r = [None, None]
        t_sg_r = [None, None]
        t_wo_r = [None, None, None, None]
        for j, (pkap, scap, kind, rb, nsi, wid) in enumerate(jobs):
            bj = j % 2
            tpk = P.dma("sync", pk[bj][:, 0:wid], pkap[rb * 128:(rb + 1) * 128, nsi * NS:nsi * NS + wid],
                        deps=later(t_pk_r[bj]), stream="dqp" + str(bj))
            ssrc = scap[rb * 8:(rb + 1) * 8, nsi * NS:nsi * NS + wid].unsqueeze(1).broadcast_to([8, 16, wid])
            tsg = P.dma("sync", sg32[bj][:, 0:wid], ssrc, deps=later(t_sg_r[bj]), stream="dqs" + str(bj))
            tsgb = P.emit("vector", lambda e, b=bj, w=wid: e.tensor_copy(sgb[:, 0:w], sg32[b][:, 0:w]),
                          deps=[tsg])
            t_sg_r[bj] = tsgb
            t80 = P.emit("vector", lambda e, b=bj, w=wid: e.tensor_scalar(
                c8[0][:, 0:w].bitcast(mybir.dt.uint32), pk[b][:, 0:w].bitcast(mybir.dt.uint32),
                0x0F0F0F0F, None, op0=ALU.bitwise_and), deps=[tpk])
            t81 = P.emit("vector", lambda e, b=bj, w=wid: e.tensor_scalar(
                c8[1][:, 0:w].bitcast(mybir.dt.uint32), pk[b][:, 0:w].bitcast(mybir.dt.uint32),
                4, 0x0F0F0F0F, op0=ALU.logical_shift_right, op1=ALU.bitwise_and), deps=[tpk])
            t_pk_r[bj] = t81
            tcbs = []
            for nib in range(2):
                tcbs.append(P.emit("vector", lambda e, w=wid, n=nib: e.tensor_copy(cbf[n][:, 0:w], c8[n][:, 0:w]),
                                   deps=[t80 if nib == 0 else t81, t_cbf_r[nib]]))
            tsns = []
            for nib in range(2):
                tsns.append(P.emit("scalar", lambda e, w=wid, n=nib: e.activation(
                    sgn[n][:, 0:w], cbf[n][:, 0:w], AF.Sign, bias=cst[:, 0:1], scale=cst[:, 2:3]),
                    deps=[tcbs[nib], t_sgn_r[nib]]))
            for nib in range(2):
                tm4 = P.emit("vector", lambda e, w=wid, n=nib: e.scalar_tensor_tensor(
                    mcb[:, 0:w], sgn[n][:, 0:w], 4.0, cbf[n][:, 0:w], op0=ALU.mult, op1=ALU.add),
                    deps=[tsns[nib]])
                tmc = P.emit("vector", lambda e, w=wid: e.tensor_scalar(mcb[:, 0:w], mcb[:, 0:w], -4.0, None, op0=ALU.add))
                tb2 = P.emit("vector", lambda e, w=wid: e.tensor_scalar(b2b[:, 0:w], mcb[:, 0:w], -2.0, None, op0=ALU.add))
                trr = P.emit("vector", lambda e, w=wid: e.tensor_scalar(rrb[:, 0:w], mcb[:, 0:w], -6.0, 0.0,
                                                                       op0=ALU.add, op1=ALU.max))
                tm1 = P.emit("vector", lambda e, w=wid: e.scalar_tensor_tensor(m1b[:, 0:w], mcb[:, 0:w], 0.5, b2b[:, 0:w],
                                                                              op0=ALU.mult, op1=ALU.max))
                tmg = P.emit("vector", lambda e, w=wid: e.tensor_tensor(m1b[:, 0:w], m1b[:, 0:w], rrb[:, 0:w], op=ALU.add))
                tvv = P.emit("vector", lambda e, w=wid, n=nib: e.tensor_tensor(sgn[n][:, 0:w], m1b[:, 0:w], sgn[n][:, 0:w], op=ALU.mult))
                t_cbf_r[nib] = tm4
                wsl = bj * 2 + nib
                two = P.emit("vector", lambda e, w=wid, b=wsl, n=nib: e.tensor_tensor(wo2[b][:, 0:w], sgn[n][:, 0:w], sgb[:, 0:w], op=ALU.mult),
                             deps=later(t_wo_r[wsl]))
                t_sgn_r[nib] = two
                if kind in ("g", "u"):
                    wdst = wgq if kind == "g" else wuq
                    dst = wdst[rb * 256:(rb + 1) * 256, nsi * NS:nsi * NS + wid].rearrange(
                        "(p two) m -> two p m", two=2)[nib]
                    tst = P.dma("scalar", dst, wo2[wsl][:, 0:wid], deps=[two], stream="dqw" + str(wsl))
                else:
                    mc2 = 2 * rb + nib
                    dst = wdq[:, :, mc2 * 128:(mc2 + 1) * 128].transpose([1, 0, 2])
                    tst = P.dma("scalar", dst, wo2[wsl][:, 0:wid].rearrange("q (hti hc) -> q hti hc", hc=128),
                                deps=[two], stream="dqw" + str(wsl))
                t_wo_r[wsl] = tst
    t_dq_done = P.barrier()

    with contextlib.ExitStack() as ctx:
        wq_s = ctx.enter_context(sb("wq_s", [128, NCH * C], BF16))
        wk_s = ctx.enter_context(sb("wk_s", [128, NCH * 512], BF16))
        wv_s = ctx.enter_context(sb("wv_s", [128, NCH * 512], BF16))
        evq = [ctx.enter_context(sb(f"p1evq{i}", [128, 512], BF16)) for i in range(4)]
        evv = [ctx.enter_context(sb(f"p1evv{i}", [128, 512], BF16)) for i in range(2)]
        p_qk = [ctx.enter_context(ps(f"p1pqk{i}", [128, 512], F32)) for i in range(4)]
        p_v = [ctx.enter_context(ps(f"p1pv{i}", [128, 512], F32)) for i in range(2)]
        twl = []
        for kc in range(NCH):
            twl.append(cast_load(wq_s[:, kc * C:(kc + 1) * C], wqT[kc * 128:(kc + 1) * 128, :],
                                 deps=(t_dq_done if kc == 0 else ()), stream="p1w"))
            twl.append(cast_load(wk_s[:, kc * 512:(kc + 1) * 512], wkT[kc * 128:(kc + 1) * 128, :], stream="p1w"))
            twl.append(cast_load(wv_s[:, kc * 512:(kc + 1) * 512], wvT[kc * 128:(kc + 1) * 128, :], stream="p1w"))
        st_p1 = {"qk": [None] * 4, "v": [None] * 2, "eq": [None] * 4, "ev": [None] * 2}

        def consume_p1(tb, xn_t, txn):
            t0 = tb * 512
            tmq = None
            for nt in range(NCH + 4):
                isq = nt < NCH
                bi = nt % 4
                w_s = wq_s if isq else wk_s
                wof = (nt * 128) if isq else ((nt - NCH) * 128)
                wstr = C if isq else 512
                for kc in range(NCH):
                    tmq = P.emit("tensor", lambda e, k=kc, b=bi, w=w_s, o=wof, s=wstr, xn=xn_t: e.matmul(
                        p_qk[b][:], w[:, k * s + o: k * s + o + 128], xn[:, k * 512:(k + 1) * 512],
                        start=(k == 0), stop=(k == NCH - 1)),
                        deps=later(txn, st_p1["qk"][bi], *(twl if tb == 0 else [])) if kc == 0 else ())
                tev = P.emit("vector", lambda e, b=bi: e.tensor_copy(evq[b][:], p_qk[b][:]),
                             deps=later(tmq, st_p1["eq"][bi]))
                st_p1["qk"][bi] = tev
                dst = qh[nt, :, t0:t0 + 512] if isq else kh[nt - NCH, :, t0:t0 + 512]
                tst = P.dma("scalar", dst, evq[bi][:], deps=[tev], stream="p1qk" + str(bi))
                st_p1["eq"][bi] = tst
            tmv = None
            for tt4 in range(4):
                bi = tt4 % 2
                for kc in range(NCH):
                    tmv = P.emit("tensor", lambda e, k=kc, s=tt4, b=bi, xn=xn_t: e.matmul(
                        p_v[b][:], xn[:, k * 512 + s * 128: k * 512 + s * 128 + 128], wv_s[:, k * 512:(k + 1) * 512],
                        start=(k == 0), stop=(k == NCH - 1)),
                        deps=later(txn, st_p1["v"][bi]) if kc == 0 else ())
                tev = P.emit("vector", lambda e, b=bi: e.tensor_copy(evv[b][:], p_v[b][:]),
                             deps=later(tmv, st_p1["ev"][bi]))
                st_p1["v"][bi] = tev
                tst = P.dma("scalar", vh[t0 + tt4 * 128: t0 + tt4 * 128 + 128, :], evv[bi][:], deps=[tev],
                            stream="p1v" + str(bi))
                st_p1["ev"][bi] = tst
            return [tmq, tmv]

        ln_pass(x_in, "p1", [], consume_p1)
    t_p1_done = P.barrier()
    if stop_after == "p1":
        P.emit("scalar", lambda e: e.nop(), deps=[(k, P.cnt[k]) for k in P.cnt])
        P.build(); glob.close(); return nc

    with contextlib.ExitStack() as ctx:
        wo_s = ctx.enter_context(sb("wo_s", [128, NCH * C], BF16))
        qq = ctx.enter_context(sb("qq", [128, NCH * 512], BF16))
        kk = ctx.enter_context(sb("kk", [128, 4 * 12 * 128], BF16))
        vaug = ctx.enter_context(sb("vaug", [128, 12 * 516], BF16))
        es = ctx.enter_context(sb("es", [128, 12 * 512], BF16))
        essk = ctx.enter_context(sb("essk", [4, 512], BF16))
        ksink = ctx.enter_context(sb("ksink", [128, 4 * 128], BF16))
        vsink = ctx.enter_context(sb("vsink", [4, 4 * 129], BF16))
        yf = ctx.enter_context(sb("yf", [128, NCH * 512], BF16))
        ytt = ctx.enter_context(sb("ytt", [128, 128], BF16))
        rcp = ctx.enter_context(sb("rcp", [128, 4], F32))
        xo = [ctx.enter_context(sb(f"p2xo{i}", [128, 512], F32)) for i in range(2)]
        xr = [ctx.enter_context(sb(f"p2xr{i}", [128, 512], F32)) for i in range(2)]
        p_sc = [ctx.enter_context(ps(f"p2sc{i}", [128, 512], F32)) for i in range(2)]
        p_av = [ctx.enter_context(ps(f"p2av{i}", [128, 512], F32)) for i in range(2)]
        p_sk = ctx.enter_context(ps("p2sk", [4, 512], F32))
        p_tr = ctx.enter_context(ps("p2tr", [128, 128], BF16))
        p_wo = [ctx.enter_context(ps(f"p2wo{i}", [128, 512], F32)) for i in range(2)]

        twl = [cast_load(wo_s[:, kc * C:(kc + 1) * C], woT[kc * 128:(kc + 1) * 128, :],
                         deps=(t_p1_done if kc == 0 else ()), stream="p2w") for kc in range(NCH)]
        tva0 = P.emit("vector", lambda e: e.memset(
            vaug[:].rearrange("p (kt h o) -> p kt h o", h=4, o=129)[:, :, :, 128], 1.0))
        tvs0 = P.emit("vector", lambda e: e.memset(
            vsink[:].rearrange("p (h o) -> p h o", o=129)[:, :, 128], 1.0))
        tks = P.dma("sync", ksink[:].rearrange("p (h t) -> p h t", t=128),
                    kh[:, :, 0:128].transpose([1, 0, 2]), deps=t_p1_done, stream="p2snk")
        tvsl = P.dma("sync", vsink[:].rearrange("p (h o) -> p h o", o=129)[:, :, 0:128],
                     vh[0:4, :].rearrange("p (h d) -> p h d", d=128), deps=later(*t_p1_done, tvs0), stream="p2snk")

        t_q_r = None
        t_kk_r = None
        t_va_r = tva0
        t_es_w = {}
        t_es_r = {}
        t_sc_free = [None, None]
        t_av_free = [None, None]
        t_sk_free = None
        t_essk_r = None
        t_tr_free = None
        t_ytt_r = None
        t_rcp_r = None
        t_yf_r = None
        t_yf_w = {}
        t_wo_free = [None, None]
        t_xo_r = [None, None]
        t_xr_r = [None, None]

        for qb in range(NTB):
            q0 = qb * 512
            kt_lo = max(0, qb * 4 - 8)
            nkt = qb * 4 + 4 - kt_lo
            tqld = P.dma("sync", qq[:].rearrange("p (h t) -> p h t", t=512),
                         qh[:, :, q0:q0 + 512].transpose([1, 0, 2]),
                         deps=later(t_q_r, *t_p1_done), stream="p2q")
            tkld = P.dma("sync", kk[:, 0:4 * nkt * 128].rearrange("p (h t) -> p h t", t=nkt * 128),
                         kh[:, :, kt_lo * 128:(kt_lo + nkt) * 128].transpose([1, 0, 2]),
                         deps=later(t_kk_r, *t_p1_done), stream="p2k")
            tvld = None
            for ki2 in range(nkt):
                tvld = P.dma("sync",
                             vaug[:, ki2 * 516:(ki2 + 1) * 516].rearrange("p (h o) -> p h o", o=129)[:, :, 0:128],
                             vh[(kt_lo + ki2) * 128:(kt_lo + ki2 + 1) * 128, :].rearrange("p (h d) -> p h d", d=128),
                             deps=later(t_va_r, *t_p1_done) if ki2 == 0 else (), stream="p2v")
            tevy = None
            for h in range(NCH):
                kvh = h // 4
                if qb >= 2:
                    tskm = P.emit("tensor", lambda e, kv=kvh, hh=h: e.matmul(
                        p_sk[:], ksink[:, kv * 128:kv * 128 + 4], qq[:, hh * 512:(hh + 1) * 512],
                        start=True, stop=True), deps=later(tks, tqld, t_sk_free))
                    tske = P.emit("scalar", lambda e: e.activation(essk[:], p_sk[:], AF.Exp, bias=cst[0:4, 1:2], scale=cst[0:4, 3:4]),
                                  deps=later(tskm, t_essk_r))
                    t_sk_free = tske
                for ki in range(nkt):
                    kt = kt_lo + ki
                    bsc = ki % 2
                    tmm = P.emit("tensor", lambda e, kv=kvh, i=ki, hh=h, b=bsc, n=nkt: e.matmul(
                        p_sc[b][:], kk[:, (kv * n + i) * 128:(kv * n + i) * 128 + 128],
                        qq[:, hh * 512:(hh + 1) * 512], start=True, stop=True),
                        deps=later(tkld, tqld, t_sc_free[bsc]))
                    tmask = tmm
                    for qs in range(4):
                        qt = qb * 4 + qs
                        if kt == qt:
                            mt = mc_t
                        elif kt == qt - 8:
                            mt = mbs_t if kt == 0 else mb_t
                        else:
                            continue
                        tmask = P.emit("vector", lambda e, b=bsc, q=qs, m=mt: e.tensor_tensor(
                            p_sc[b][:, q * 128:(q + 1) * 128], p_sc[b][:, q * 128:(q + 1) * 128], m[:], op=ALU.add),
                            deps=[tmask])
                    texp = P.emit("scalar", lambda e, b=bsc, i=ki: e.activation(
                        es[:, i * 512:(i + 1) * 512], p_sc[b][:], AF.Exp, bias=cst[:, 1:2], scale=cst[:, 3:4]),
                        deps=later(tmask, t_es_r.get(ki)))
                    t_sc_free[bsc] = texp
                    t_es_w[ki] = texp
                for qs in range(4):
                    qt = qb * 4 + qs
                    bav = qs % 2
                    col = (qs // 2) * 258
                    valid = list(range(max(kt_lo, qt - 8), qt + 1))
                    has_sink = qt >= 9
                    tav = None
                    for vi, kt in enumerate(valid):
                        ki = kt - kt_lo
                        is_last = (vi == len(valid) - 1) and not has_sink
                        tav = P.emit("tensor", lambda e, i=ki, q=qs, kv=kvh, b=bav, cl=col,
                                     stt=(vi == 0), sp=is_last: e.matmul(
                            p_av[b][:, cl:cl + 129], es[:, i * 512 + q * 128: i * 512 + q * 128 + 128],
                            vaug[:, i * 516 + kv * 129: i * 516 + kv * 129 + 129],
                            start=stt, stop=sp),
                            deps=later(t_es_w[ki], tvld, t_av_free[bav]) if vi == 0 else [t_es_w[ki]])
                        t_es_r[ki] = tav
                    if has_sink:
                        tav = P.emit("tensor", lambda e, q=qs, kv=kvh, b=bav, cl=col: e.matmul(
                            p_av[b][:, cl:cl + 129], essk[:, q * 128:(q + 1) * 128],
                            vsink[:, kv * 129:(kv + 1) * 129], start=False, stop=True),
                            deps=later(t_sk_free, tvsl))
                        t_essk_r = tav
                    trc = P.emit("vector", lambda e, b=bav, cl=col, q=qs: e.reciprocal(
                        rcp[:, q:q + 1], p_av[b][:, cl + 128:cl + 129]), deps=later(tav, t_rcp_r))
                    tsc2 = P.emit("scalar", lambda e, b=bav, cl=col, q=qs: e.activation(
                        ytt[:], p_av[b][:, cl:cl + 128], AF.Copy, scale=rcp[:, q:q + 1]),
                        deps=later(trc, t_ytt_r))
                    t_av_free[bav] = tsc2
                    t_rcp_r = tsc2
                    ttr = P.emit("tensor", lambda e: e.transpose(p_tr[:], ytt[:], ident[:]),
                                 deps=later(tsc2, t_tr_free))
                    t_ytt_r = ttr
                    tevy = P.emit("vector", lambda e, hh=h, q=qs: e.tensor_copy(
                        yf[:, hh * 512 + q * 128: hh * 512 + q * 128 + 128], p_tr[:]),
                        deps=later(ttr, t_yf_r if (h, qs) == (0, 0) else None))
                    t_tr_free = tevy
                    t_yf_w[(h, qs)] = tevy
            t_q_r = tevy
            t_kk_r = tevy
            t_va_r = tevy
            yf_deps = list(t_yf_w.values())
            tmo = None
            for nt in range(NCH):
                bwo = nt % 2
                txl = P.dma("sync", xr[bwo][:], x_in[nt * 128:(nt + 1) * 128, q0:q0 + 512],
                            deps=later(t_xr_r[bwo], *(t_p1_done if qb == 0 and nt < 2 else [])),
                            stream="p2xr" + str(bwo))
                for kc in range(NCH):
                    tmo = P.emit("tensor", lambda e, k=kc, n=nt, b=bwo: e.matmul(
                        p_wo[b][:], wo_s[:, k * C + n * 128: k * C + n * 128 + 128],
                        yf[:, k * 512:(k + 1) * 512], start=(k == 0), stop=(k == NCH - 1)),
                        deps=later(*yf_deps, *(twl if qb == 0 and nt == 0 else []), t_wo_free[bwo]) if kc == 0 else ())
                tad = P.emit("vector", lambda e, b=bwo: e.tensor_tensor(xo[b][:], p_wo[b][:], xr[b][:], op=ALU.add),
                             deps=later(tmo, txl, t_xo_r[bwo]))
                t_wo_free[bwo] = tad
                t_xr_r[bwo] = tad
                tso = P.dma("scalar", xbb[nt * 128:(nt + 1) * 128, q0:q0 + 512], xo[bwo][:], deps=[tad],
                            stream="p2st" + str(bwo))
                t_xo_r[bwo] = tso
            t_yf_r = tmo
            t_yf_w = {}
    t_p2_done = P.barrier()
    if stop_after == "p2":
        tdmp = P.dma("gpsimd", out, xbb, deps=[(k, P.cnt[k]) for k in P.cnt], stream="dump")
        P.emit("scalar", lambda e: e.nop(), deps=[tdmp])
        P.build(); glob.close(); return nc

    def ffn_phase(xsrc, xdst, tag, pre):
        with contextlib.ExitStack() as ctx:
            with contextlib.ExitStack() as c2:
                xn_all = c2.enter_context(sb(f"{tag}xna", [128, NCH * T], BF16))
                tbds = []

                def consume_ffn(tb, xbuf, tok):
                    tlast = None
                    for kc in range(NCH):
                        tlast = P.emit("vector", lambda e, k=kc, xb_=xbuf, t0=tb * 512: e.tensor_copy(
                            xn_all[:, k * T + t0: k * T + t0 + 512], xb_[:, k * 512:(k + 1) * 512]),
                            deps=[tok] if kc == 0 else ())
                    tbds.append(tlast)
                    return tlast

                ln_pass(xsrc, tag + "l", pre, consume_ffn)
                wg_t = [c2.enter_context(sb(f"{tag}wg{i}", [128, NCH * 128], BF16)) for i in range(2)]
                wu_t = [c2.enter_context(sb(f"{tag}wu{i}", [128, NCH * 128], BF16)) for i in range(2)]
                gb = c2.enter_context(sb(f"{tag}gb", [128, 512], BF16))
                ub = c2.enter_context(sb(f"{tag}ub", [128, 512], BF16))
                geb = c2.enter_context(sb(f"{tag}geb", [128, 512], BF16))
                ib = [c2.enter_context(sb(f"{tag}ib{i}", [128, NTB * 512], BF16)) for i in range(2)]
                p_g = [c2.enter_context(ps(f"{tag}pg{i}", [128, 512], F32)) for i in range(2)]
                p_u = [c2.enter_context(ps(f"{tag}pu{i}", [128, 512], F32)) for i in range(2)]
                t_wg_r = [None, None]
                t_wu_r = [None, None]
                t_pg_free = [None, None]
                t_pu_free = [None, None]
                t_ib_r = [None, None]
                for nt in range(128):
                    bw = nt % 2
                    gate0 = (list(t_dq_done) + list(pre) + [tbds[-1]]) if nt < 2 else []
                    twg = P.dma("sync", wg_t[bw][:].rearrange("p (kc m) -> p kc m", m=128),
                                wgq[:, nt * 128:(nt + 1) * 128].rearrange("(kc p) m -> p kc m", p=128),
                                deps=later(t_wg_r[bw], *gate0), stream="ffwg" + str(bw))
                    twu = P.dma("sync", wu_t[bw][:].rearrange("p (kc m) -> p kc m", m=128),
                                wuq[:, nt * 128:(nt + 1) * 128].rearrange("(kc p) m -> p kc m", p=128),
                                deps=later(t_wu_r[bw], *gate0), stream="ffwu" + str(bw))
                    tibd = None
                    for tb in range(NTB):
                        bp = tb % 2
                        tg = tu = None
                        for kc in range(NCH):
                            tg = P.emit("tensor", lambda e, k=kc, b=bw, p=bp, t0=tb * 512: e.matmul(
                                p_g[p][:], wg_t[b][:, k * 128:(k + 1) * 128],
                                xn_all[:, k * T + t0: k * T + t0 + 512],
                                start=(k == 0), stop=(k == NCH - 1)),
                                deps=later(twg, tbds[tb], t_pg_free[bp]) if kc == 0 else ())
                        for kc in range(NCH):
                            tu = P.emit("tensor", lambda e, k=kc, b=bw, p=bp, t0=tb * 512: e.matmul(
                                p_u[p][:], wu_t[b][:, k * 128:(k + 1) * 128],
                                xn_all[:, k * T + t0: k * T + t0 + 512],
                                start=(k == 0), stop=(k == NCH - 1)),
                                deps=later(twu, tbds[tb], t_pu_free[bp]) if kc == 0 else ())
                        tgb = P.emit("vector", lambda e, p=bp: e.tensor_copy(gb[:], p_g[p][:]), deps=[tg])
                        t_pg_free[bp] = tgb
                        tge = P.emit("scalar", lambda e: e.activation(geb[:], gb[:], AF.Gelu_apprx_tanh),
                                     deps=[tgb])
                        tub = P.emit("vector", lambda e, p=bp: e.tensor_copy(ub[:], p_u[p][:]), deps=[tu])
                        t_pu_free[bp] = tub
                        tibd = P.emit("vector", lambda e, b=bw, t=tb: e.tensor_tensor(
                            ib[b][:, t * 512:(t + 1) * 512], geb[:], ub[:], op=ALU.mult),
                            deps=later(tge, t_ib_r[bw] if tb == 0 else None))
                    tist = P.dma("scalar", inter[nt * 128:(nt + 1) * 128, :], ib[bw][:], deps=[tibd],
                                 stream="ffis" + str(bw))
                    t_ib_r[bw] = tist
                    t_wg_r[bw] = tg
                    t_wu_r[bw] = tu
            t_s2a = P.barrier()
            with contextlib.ExitStack() as c2:
                it_all = c2.enter_context(sb(f"{tag}ita", [128, 128 * 512], BF16))
                wd_t = c2.enter_context(sb(f"{tag}wd", [128, MLP], BF16))
                xrr = [c2.enter_context(sb(f"{tag}xr{i}", [128, 512], F32)) for i in range(2)]
                xoo = [c2.enter_context(sb(f"{tag}xo{i}", [128, 512], F32)) for i in range(2)]
                p_d = [c2.enter_context(ps(f"{tag}pd{i}", [128, 512], F32)) for i in range(2)]
                t_it_r = None
                t_wd_r = [None, None]
                t_pd_free = [None, None]
                t_xr_r2 = [None, None]
                t_xo_r2 = [None, None]
                for tb in range(NTB):
                    tits = []
                    for g8 in range(8):
                        tits.append(P.dma("sync",
                                          it_all[:, g8 * 16 * 512:(g8 + 1) * 16 * 512].rearrange("p (m t) -> p m t", t=512),
                                          inter[g8 * C:(g8 + 1) * C, tb * 512:(tb + 1) * 512].rearrange("(m p) t -> p m t", p=128),
                                          deps=later(t_it_r, *t_s2a), stream="ffit"))
                    for hti in range(NCH):
                        bp = hti % 2
                        gate1 = (list(t_dq_done) + list(t_s2a)) if tb == 0 and hti == 0 else []
                        tw0 = P.dma("sync", wd_t[:, 0:8192], wdq[hti][:, 0:8192],
                                    deps=later(t_wd_r[0], *gate1), stream="ffwdA")
                        tw1 = P.dma("sync", wd_t[:, 8192:MLP], wdq[hti][:, 8192:MLP],
                                    deps=later(t_wd_r[1], *gate1), stream="ffwdB")
                        tdm = None
                        for mc2 in range(128):
                            half_tok = tw1 if mc2 == 64 else None
                            tdm = P.emit("tensor", lambda e, m=mc2, p=bp: e.matmul(
                                p_d[p][:], wd_t[:, m * 128:(m + 1) * 128], it_all[:, m * 512:(m + 1) * 512],
                                start=(m == 0), stop=(m == 127)),
                                deps=later(tw0, *tits, t_pd_free[bp]) if mc2 == 0 else later(half_tok))
                            if mc2 == 63:
                                t_wd_r[0] = tdm
                            if mc2 == 127:
                                t_wd_r[1] = tdm
                        txl = P.dma("sync", xrr[bp][:], xsrc[hti * 128:(hti + 1) * 128, tb * 512:(tb + 1) * 512],
                                    deps=later(t_xr_r2[bp], *(t_s2a if tb == 0 and hti < 2 else [])),
                                    stream="ffxr" + str(bp))
                        tad = P.emit("vector", lambda e, b=bp: e.tensor_tensor(
                            xoo[b][:], p_d[b][:], xrr[b][:], op=ALU.add),
                            deps=later(tdm, txl, t_xo_r2[bp]))
                        t_pd_free[bp] = tad
                        t_xr_r2[bp] = tad
                        tso = P.dma("scalar", xdst[hti * 128:(hti + 1) * 128, tb * 512:(tb + 1) * 512], xoo[bp][:],
                                    deps=[tad], stream="ffst" + str(bp))
                        t_xo_r2[bp] = tso
                    t_it_r = tdm
        return P.barrier()

    t_p3_done = ffn_phase(xbb, xbb, "f1", t_p2_done)
    if stop_after == "p3":
        tdmp = P.dma("gpsimd", out, xbb, deps=[(k, P.cnt[k]) for k in P.cnt], stream="dump")
        P.emit("scalar", lambda e: e.nop(), deps=[tdmp])
        P.build(); glob.close(); return nc

    with contextlib.ExitStack() as ctx:
        xst = ctx.enter_context(sb("p4xst", [128, NCH * 512], BF16))
        st4 = {"r": None}

        def consume_p4(tb, xbuf, tok):
            t0 = tb * 512
            tcp = P.emit("vector", lambda e, xb_=xbuf: e.tensor_copy(xst[:], xb_[:]),
                         deps=later(tok, st4["r"]))
            tst = P.dma("scalar", xnh[:, t0:t0 + 512].rearrange("(kc p) t -> p kc t", p=128),
                        xst[:].rearrange("p (kc t) -> p kc t", t=512), deps=[tcp], stream="p4xnh")
            st4["r"] = tst
            return tcp

        ln_pass(xbb, "p4l", t_p3_done, consume_p4)
    t_ln4 = P.barrier()

    with contextlib.ExitStack() as ctx:
        lora1 = ctx.enter_context(sb("lora1", [32, 5 * T], BF16))
        w1_s = ctx.enter_context(sb("w1_s", [128, NCH * 160], BF16))
        w2_s = ctx.enter_context(sb("w2_s", [32, 5 * C], BF16))
        dw1_s = ctx.enter_context(sb("dw1_s", [128, NCH * 64], BF16))
        dw2_s = ctx.enter_context(sb("dw2_s", [64, C], BF16))
        tw1 = cast_load(w1_s[:].rearrange("p (kc m) -> p kc m", m=160), w1.rearrange("(kc p) m -> p kc m", p=128),
                        deps=t_ln4, stream="p4w1")
        tw2 = cast_load(w2_s[:].rearrange("p (f c) -> p f c", c=C), w2.transpose([1, 0, 2]), deps=t_ln4, stream="p4w2")
        tdw1 = cast_load(dw1_s[:].rearrange("p (kc m) -> p kc m", m=64), dw1.rearrange("(kc p) m -> p kc m", p=128),
                         deps=t_ln4, stream="p4w3")
        tdw2 = cast_load(dw2_s[:], dw2, deps=t_ln4, stream="p4w4")

        def xn_load(buf, t0, deps, stream):
            if t0 == 0:
                return P.dma("sync", buf[:].rearrange("p (kc t) -> p kc t", t=513)[:, :, 1:513],
                             xnh[:, 0:512].rearrange("(kc p) t -> p kc t", p=128),
                             deps=deps, stream=stream)
            return P.dma("sync", buf[:].rearrange("p (kc t) -> p kc t", t=513),
                         xnh[:, t0 - 1:t0 + 512].rearrange("(kc p) t -> p kc t", p=128),
                         deps=deps, stream=stream)

        with contextlib.ExitStack() as c2:
            xtl = [c2.enter_context(sb(f"s3ax{i}", [128, NCH * 513], BF16)) for i in range(2)]
            xxb = c2.enter_context(sb("s3axx", [128, 512], BF16))
            xxx = c2.enter_context(sb("s3am", [128, NCH * 512], BF16))
            p_l = c2.enter_context(ps("s3apl", [32, 512], F32))
            t_pl_free = None
            tms0 = P.emit("vector", lambda e: e.memset(
                xtl[0][:].rearrange("p (kc t) -> p kc t", t=513)[:, :, 0:1], 0.0), deps=t_ln4)
            tms1 = P.emit("vector", lambda e: e.memset(
                xtl[1][:].rearrange("p (kc t) -> p kc t", t=513)[:, :, 0:1], 0.0))
            t_x_r = [[tms0], [tms1]]
            t_xxx_pe = None
            for tb in range(NTB):
                t0 = tb * 512
                b = tb % 2
                tld = xn_load(xtl[b], t0, later(*t_x_r[b], *(t_ln4 if tb < 2 else [])), "s3ax" + str(b))
                txx = None
                for kc in range(NCH):
                    ta = P.emit("vector", lambda e, k=kc, bb=b: e.tensor_sub(
                        xxb[:], xtl[bb][:, k * 513: k * 513 + 512], xtl[bb][:, k * 513 + 1: k * 513 + 513]),
                        deps=[tld])
                    if t0 == 0:
                        ta = P.emit("vector", lambda e, k=kc, bb=b: e.tensor_scalar(
                            xxb[:, 0:1], xtl[bb][:, k * 513 + 1: k * 513 + 2], -1.0, None, op0=ALU.mult))
                    txx = P.emit("vector", lambda e, k=kc, bb=b: e.scalar_tensor_tensor(
                        xxx[:, k * 512:(k + 1) * 512], xxb[:], maa_s[:, 0 * NCH + k: 0 * NCH + k + 1],
                        xtl[bb][:, k * 513 + 1: k * 513 + 513], op0=ALU.mult, op1=ALU.add),
                        deps=later(t_xxx_pe if kc == 0 else None))
                for f in range(5):
                    tl = None
                    for kc in range(NCH):
                        tl = P.emit("tensor", lambda e, k=kc, ff=f: e.matmul(
                            p_l[:], w1_s[:, k * 160 + ff * 32: k * 160 + ff * 32 + 32],
                            xxx[:, k * 512:(k + 1) * 512],
                            start=(k == 0), stop=(k == NCH - 1)),
                            deps=later(txx, tw1, t_pl_free) if kc == 0 else ())
                    tth = P.emit("scalar", lambda e, ff=f, tt0=t0: e.activation(
                        lora1[:, ff * T + tt0: ff * T + tt0 + 512], p_l[:], AF.Tanh), deps=[tl])
                    t_pl_free = tth
                t_xxx_pe = tl
                t_x_r[b] = [tl]
        t_s3a = P.barrier()

        def emit_proj(proj_f, maa_row, wsrc, mode, dst, tagp, prev):
            with contextlib.ExitStack() as c2:
                xtl = [c2.enter_context(sb(f"{tagp}x{i}", [128, NCH * 513], BF16)) for i in range(2)]
                mixb = c2.enter_context(sb(f"{tagp}mx", [128, NCH * 512], BF16))
                xxb2 = c2.enter_context(sb(f"{tagp}xx", [128, 512], BF16))
                coef = c2.enter_context(sb(f"{tagp}cf", [128, 512], F32))
                ev = [c2.enter_context(sb(f"{tagp}ev{i}", [128, 512], BF16)) for i in range(2)]
                wt = None
                if mode == "fm":
                    wt = [c2.enter_context(sb(f"{tagp}w{i}", [128, NCH * 128], BF16)) for i in range(2)]
                elif mode == "tok":
                    wt = [c2.enter_context(sb(f"{tagp}w{i}", [128, NCH * 512], BF16)) for i in range(2)]
                ev32 = c2.enter_context(sb(f"{tagp}e32", [128, 512], F32)) if mode == "e" else None
                tnh = c2.enter_context(sb(f"{tagp}th", [64, 512], BF16)) if mode == "e" else None
                p_m = [c2.enter_context(ps(f"{tagp}pm{i}", [128, 512], F32)) for i in range(2)]
                p_p = [c2.enter_context(ps(f"{tagp}pp{i}", [128, 512], F32)) for i in range(2)]
                p_w1 = c2.enter_context(ps(f"{tagp}pw", [64, 512], F32)) if mode == "e" else None
                t_pm_free = [None, None]
                t_mix_pe = None
                t_pp_free = [None, None]
                t_ev_r = [None, None]
                t_e32_r = None
                t_coef_r = None
                t_w1_free = None
                tms0 = P.emit("vector", lambda e: e.memset(
                    xtl[0][:].rearrange("p (kc t) -> p kc t", t=513)[:, :, 0:1], 0.0), deps=prev)
                tms1 = P.emit("vector", lambda e: e.memset(
                    xtl[1][:].rearrange("p (kc t) -> p kc t", t=513)[:, :, 0:1], 0.0))
                t_x_r = [[tms0], [tms1]]
                t_w_r = [None, None]
                tlast = None
                for tb in range(NTB):
                    t0 = tb * 512
                    b = tb % 2
                    tldx = xn_load(xtl[b], t0, later(*t_x_r[b], *(prev if tb < 2 else [])), "pjx" + str(b))
                    tmx = None
                    for kc in range(NCH):
                        bm = kc % 2
                        tmk = P.emit("tensor", lambda e, f=proj_f, k=kc, tt0=t0, b=bm: e.matmul(
                            p_m[b][:, 0:512], w2_s[:, f * C + k * 128: f * C + (k + 1) * 128],
                            lora1[:, f * T + tt0: f * T + tt0 + 512], start=True, stop=True),
                            deps=later(tw2, t_pm_free[bm], *t_s3a))
                        tcf = P.emit("vector", lambda e, mr=maa_row, k=kc, b=bm: e.tensor_scalar(
                            coef[:], p_m[b][:, 0:512], maa_s[:, mr * NCH + k: mr * NCH + k + 1], None, op0=ALU.add),
                            deps=later(tmk, t_coef_r))
                        t_pm_free[bm] = tcf
                        ta = P.emit("vector", lambda e, k=kc, bb=b: e.tensor_sub(
                            xxb2[:], xtl[bb][:, k * 513: k * 513 + 512], xtl[bb][:, k * 513 + 1: k * 513 + 513]),
                            deps=[tldx])
                        if t0 == 0:
                            ta = P.emit("vector", lambda e, k=kc, bb=b: e.tensor_scalar(
                                xxb2[:, 0:1], xtl[bb][:, k * 513 + 1: k * 513 + 2], -1.0, None, op0=ALU.mult))
                        P.emit("vector", lambda e, k=kc: e.tensor_tensor(
                            mixb[:, k * 512:(k + 1) * 512], xxb2[:], coef[:], op=ALU.mult),
                            deps=later(t_mix_pe if kc == 0 else None))
                        tmx = P.emit("vector", lambda e, k=kc, bb=b: e.tensor_tensor(
                            mixb[:, k * 512:(k + 1) * 512], mixb[:, k * 512:(k + 1) * 512],
                            xtl[bb][:, k * 513 + 1: k * 513 + 513], op=ALU.add))
                    t_x_r[b] = [tmx]
                    if mode == "fm":
                        for nt2 in range(NCH):
                            bp = nt2 % 2
                            twp = P.dma("sync", wt[bp][:].rearrange("p (kc m) -> p kc m", m=128),
                                        wsrc[:, nt2 * 128:(nt2 + 1) * 128].rearrange("(kc p) m -> p kc m", p=128),
                                        deps=later(t_w_r[bp], tBW[4],
                                                   *(prev if tb == 0 and nt2 < 2 else [])),
                                        stream="pjw" + str(bp))
                            tp = None
                            for kc in range(NCH):
                                tp = P.emit("tensor", lambda e, k=kc, p=bp: e.matmul(
                                    p_p[p][:], wt[p][:, k * 128:(k + 1) * 128],
                                    mixb[:, k * 512:(k + 1) * 512], start=(k == 0), stop=(k == NCH - 1)),
                                    deps=later(tmx, t_pp_free[bp], twp) if kc == 0 else ())
                            t_w_r[bp] = tp
                            tev = P.emit("vector", lambda e, p=bp: e.tensor_copy(ev[p][:], p_p[p][:]),
                                         deps=later(tp, t_ev_r[bp]))
                            t_pp_free[bp] = tev
                            ck0 = t0 // 128
                            dsta = dst[ck0:ck0 + 4, :, :]
                            ts1 = P.dma("scalar",
                                        dsta[:, :, (2 * nt2) * 128:(2 * nt2) * 128 + 128].transpose([1, 0, 2]),
                                        ev[bp][0:64, :].rearrange("p (cc t) -> p cc t", t=128),
                                        deps=[tev], stream="pjs" + str(bp))
                            ts2 = P.dma("scalar",
                                        dsta[:, :, (2 * nt2 + 1) * 128:(2 * nt2 + 1) * 128 + 128].transpose([1, 0, 2]),
                                        ev[bp][64:128, :].rearrange("p (cc t) -> p cc t", t=128),
                                        deps=[tev], stream="pjs" + str(bp))
                            t_ev_r[bp] = ts2
                            tlast = ts2
                            t_mix_pe = tp
                    elif mode == "tok":
                        for ng in range(4):
                            bp = ng % 2
                            twp = P.dma("sync", wt[bp][:].rearrange("p (kc m) -> p kc m", m=512),
                                        wsrc[:, ng * 512:(ng + 1) * 512].rearrange("(kc p) m -> p kc m", p=128),
                                        deps=later(t_w_r[bp], tBW[4],
                                                   *(prev if tb == 0 and ng < 2 else [])),
                                        stream="pjw" + str(bp))
                            tp = None
                            for tt4 in range(4):
                                for kc in range(NCH):
                                    tp = P.emit("tensor", lambda e, k=kc, s=tt4, p=bp: e.matmul(
                                        p_p[p][:], mixb[:, k * 512 + s * 128: k * 512 + s * 128 + 128],
                                        wt[p][:, k * 512:(k + 1) * 512],
                                        start=(k == 0), stop=(k == NCH - 1)),
                                        deps=later(tmx, t_pp_free[bp], twp) if kc == 0 else ())
                                if dst is gg_h:
                                    tev = P.emit("scalar", lambda e, p=bp: e.activation(ev[p][:], p_p[p][:], AF.Silu),
                                                 deps=later(tp, t_ev_r[bp]))
                                else:
                                    tev = P.emit("vector", lambda e, p=bp: e.tensor_copy(ev[p][:], p_p[p][:]),
                                                 deps=later(tp, t_ev_r[bp]))
                                t_pp_free[bp] = tev
                                ts1 = P.dma("scalar", dst[t0 + tt4 * 128: t0 + tt4 * 128 + 128, ng * 512:(ng + 1) * 512],
                                            ev[bp][:], deps=[tev], stream="pjs" + str(bp))
                                t_ev_r[bp] = ts1
                                tlast = ts1
                                t_mix_pe = tp
                            t_w_r[bp] = tp
                    else:
                        tq1 = None
                        for kc in range(NCH):
                            tq1 = P.emit("tensor", lambda e, k=kc: e.matmul(
                                p_w1[:], dw1_s[:, k * 64:(k + 1) * 64], mixb[:, k * 512:(k + 1) * 512],
                                start=(k == 0), stop=(k == NCH - 1)),
                                deps=later(tmx, tdw1, t_w1_free) if kc == 0 else ())
                        tth2 = P.emit("scalar", lambda e: e.activation(tnh[:], p_w1[:], AF.Tanh), deps=[tq1])
                        t_w1_free = tth2
                        t_mix_pe = tq1
                        for nt2 in range(NCH):
                            bp = nt2 % 2
                            tp = P.emit("tensor", lambda e, n=nt2, p=bp: e.matmul(
                                p_p[p][:], dw2_s[:, n * 128:(n + 1) * 128], tnh[:],
                                start=True, stop=True), deps=later(tth2, tdw2, t_pp_free[bp]))
                            tcf2 = P.emit("vector", lambda e, p=bp, n=nt2: e.tensor_scalar(
                                coef[:], p_p[p][:], maa_s[:, 6 * NCH + n: 6 * NCH + n + 1], None, op0=ALU.add),
                                deps=later(tp, t_coef_r))
                            t_pp_free[bp] = tcf2
                            tex = P.emit("scalar", lambda e: e.activation(ev32[:], coef[:], AF.Exp),
                                         deps=later(tcf2, t_e32_r))
                            t_coef_r = tex
                            ck0 = t0 // 128
                            dsta = dst[ck0:ck0 + 4, :, :]
                            ts1 = P.dma("scalar",
                                        dsta[:, :, (2 * nt2) * 128:(2 * nt2) * 128 + 128].transpose([1, 0, 2]),
                                        ev32[0:64, :].rearrange("p (cc t) -> p cc t", t=128),
                                        deps=[tex], stream="pjs0")
                            ts2 = P.dma("scalar",
                                        dsta[:, :, (2 * nt2 + 1) * 128:(2 * nt2 + 1) * 128 + 128].transpose([1, 0, 2]),
                                        ev32[64:128, :].rearrange("p (cc t) -> p cc t", t=128),
                                        deps=[tex], stream="pjs0")
                            t_e32_r = ts2
                            tlast = ts2
                return tlast

        emit_proj(0, 1, None, "e", ee_h, "pw_", t_s3a)
        pb1 = P.barrier()
        emit_proj(3, 4, brB, "fm", rr_h, "pr_", pb1)
        pb2 = P.barrier()
        emit_proj(1, 2, bkB, "fm", kk_h, "pk_", pb2)
        pb3 = P.barrier()
        emit_proj(2, 3, bvB, "tok", vb_h, "pv_", pb3)
        pb4 = P.barrier()
        emit_proj(4, 5, bgB, "tok", gg_h, "pg_", pb4)
        t_s3b = P.barrier()
    if stop_after.startswith("s3b"):
        which = {"s3b_r": rr_h, "s3b_k": kk_h, "s3b_e": ee_h, "s3b_v": vb_h, "s3b_g": gg_h,
                 "s3b_x": xnh}.get(stop_after, ee_h)
        if which in (vb_h, gg_h):
            dsrc = which.rearrange("t c -> c t") if False else which
            tdmp = P.dma("gpsimd", out.rearrange("c t -> t c") if False else out, which.rearrange("t (a b) -> (t a) b", a=1) if False else which.rearrange("t c -> (t c)").rearrange("(a b) -> a b", a=2048), deps=[(k, P.cnt[k]) for k in P.cnt], stream="dump")
        elif which is xnh:
            tdmp = P.dma("gpsimd", out, which, deps=[(k, P.cnt[k]) for k in P.cnt], stream="dump")
        else:
            tdmp = P.dma("gpsimd", out, which.rearrange("n d m -> (n d) m"), deps=[(k, P.cnt[k]) for k in P.cnt], stream="dump")
        P.emit("scalar", lambda e: e.nop(), deps=[tdmp])
        P.build(); glob.close(); return nc

    with contextlib.ExitStack() as c2:
        uT_s = c2.enter_context(sb("u_s", [WD, WH], F32))
        lnw_b = c2.enter_context(sb("lnw_b", [128, 2 * C], BF16))
        lnrow = c2.enter_context(sb("lnrow", [1, 2 * C], BF16))
        tu = P.dma("sync", uT_s[:], uT, deps=t_s3b, stream="p4u")
        tlr = cast_load(lnrow[:], lnwb.rearrange("r c -> (r c)").unsqueeze(0), deps=t_s3b, stream="p4lr")
        rr1 = c2.enter_context(sb("rr1", [WD, WH * 128], BF16))
        kk1 = c2.enter_context(sb("kk1", [WD, WH * 128], BF16))
        ee1 = c2.enter_context(sb("ee1", [WD, WH * 128], F32))
        vv1b = [c2.enter_context(sb(f"vv1{i}", [128, C], BF16)) for i in range(2)]
        g1b = [c2.enter_context(sb(f"g1{i}", [128, C], BF16)) for i in range(2)]
        scn = c2.enter_context(sb("scn", [WD, WH * 128], F32))
        sxc = c2.enter_context(sb("sxc", [WD, WH * 128], F32))
        plc = c2.enter_context(sb("plc", [WD, WH], F32))
        rt = c2.enter_context(sb("rt", [WD, WH * 128], BF16))
        kt_ = c2.enter_context(sb("kt_", [WD, WH * 128], BF16))
        td_ = c2.enter_context(sb("td_", [WD, WH * 128], BF16))
        zerw = c2.enter_context(sb("zerw", [WD, 128], F32))
        atm4 = c2.enter_context(sb("atm4", [128, 512], BF16))
        ktr = c2.enter_context(sb("ktr", [128, WD], BF16))
        ddc = c2.enter_context(sb("ddc", [128, 1], F32))
        vdd = c2.enter_context(sb("vdd", [128, WD], BF16))
        Sst = c2.enter_context(sb("Sst", [WD, WH * WD], F32))
        Sbf = c2.enter_context(sb("Sbf", [WD, WH * WD], BF16))
        uxp = c2.enter_context(sb("uxp", [WD, WH * 128], BF16))
        onesw = c2.enter_context(sb("onesw", [WD, 128], BF16))
        sqg = c2.enter_context(sb("sqg", [128, C], BF16))
        gnr = c2.enter_context(sb("gnr", [128, 4 * WH], F32))
        zt = c2.enter_context(sb("zt", [128, C], BF16))
        zf = c2.enter_context(sb("zf", [128, NCH * 512], BF16))
        wob = [c2.enter_context(sb(f"wob{i}", [128, NCH * 128], BF16)) for i in range(2)]
        xrr = [c2.enter_context(sb(f"wvxr{i}", [128, 512], F32)) for i in range(2)]
        xoo = [c2.enter_context(sb(f"wvxo{i}", [128, 512], F32)) for i in range(2)]
        otg = [c2.enter_context(sb(f"ot_g{i}", [128, C], F32)) for i in range(2)]
        p_at = [c2.enter_context(ps(f"p_at{i}", [128, 512], F32)) for i in range(2)]
        p_out = [c2.enter_context(ps(f"p_out{i}", [128, 512], F32)) for i in range(2)]
        p_mis = [c2.enter_context(ps(f"p_mis{i}", [128, 512], F32)) for i in range(2)]
        p_tra = c2.enter_context(ps("p_tra", [128, 192], BF16))
        p_trb = c2.enter_context(ps("p_trb", [128, 128], BF16))

        tz = P.emit("vector", lambda e: e.memset(zerw[:], 0.0))
        P.emit("vector", lambda e: e.memset(Sst[:], 0.0))
        tSb = P.emit("vector", lambda e: e.memset(Sbf[:], 0.0))
        tow = P.emit("vector", lambda e: e.memset(onesw[:], 1.0))
        tux = None
        for h in range(WH):
            tux = P.emit("vector", lambda e, hh=h: e.tensor_scalar(
                uxp[:, hh * 128:(hh + 1) * 128], onesw[:], uT_s[:, hh:hh + 1], None, op0=ALU.mult),
                deps=later(tu, *t_s3b) if h == 0 else ())
        tprev = None
        tlnb = None
        for seg in range(2 * C // 512):
            tm = P.emit("tensor", lambda e, s=seg: e.matmul(
                p_at[0][:], ones1[:], lnrow[:, s * 512:(s + 1) * 512], start=True, stop=True),
                deps=later(tlr, tprev, *tCONST, *t_s3b))
            tlnb = P.emit("vector", lambda e, s=seg: e.tensor_copy(lnw_b[:, s * 512:(s + 1) * 512], p_at[0][:]),
                          deps=[tm])
            tprev = tlnb

        t_ld_r = [None] * 3
        t_ld_r2 = [None, None]
        t_g_r = [None, None]
        t_ot_r = [None, None]
        t_scn_r = None
        t_at_free = [tlnb, None]
        t_atm_pe = None
        t_out_evac = [None, None]
        t_tra_free = None
        t_trb_free = None
        t_out_free = [None, None]
        t_mis_free = [None, None]
        t_rtkt_pe = None
        t_Sbf_pe = None
        tSbf = tSb
        t_plc_r = None
        t_sxc_r = None
        t_gn_act = None
        t_zt_pe = None
        t_zf_pe = None
        t_wo_free = [None, None]
        t_wob_r = [None, None]
        t_xr_r3 = [None, None]
        t_xo_r3 = [None, None]

        NCKt = T // 128

        def stage2(rec):
            nonlocal t_gn_act, t_zt_pe, t_zf_pe, t_tra_free, t_ot_r, t_g_r, tzf_done
            ck, tld_g, g1, otb, tev8l, ccc = rec
            tsq2 = P.emit("scalar", lambda e, o=otb: e.activation(sqg[:], o[:], AF.Square),
                          deps=later(tev8l, t_gn_act))
            tgs = P.emit("vector", lambda e, o=otb: e.reduce_sum(
                gnr[:, 0:WH].rearrange("p (s o) -> p s o", o=1),
                o[:].rearrange("p (s o) -> p s o", o=64), axis=AX), deps=[tsq2])
            P.emit("vector", lambda e: e.reduce_sum(
                gnr[:, WH:2 * WH].rearrange("p (s o) -> p s o", o=1),
                sqg[:].rearrange("p (s o) -> p s o", o=64), axis=AX))
            P.emit("vector", lambda e: e.tensor_scalar(
                gnr[:, 2 * WH:3 * WH], gnr[:, 0:WH], -1.0 / 64, None, op0=ALU.mult))
            P.emit("vector", lambda e: e.tensor_tensor(
                gnr[:, 3 * WH:4 * WH], gnr[:, 2 * WH:3 * WH], gnr[:, 2 * WH:3 * WH], op=ALU.mult))
            P.emit("vector", lambda e: e.scalar_tensor_tensor(
                gnr[:, WH:2 * WH], gnr[:, WH:2 * WH], 1.0 / 64, gnr[:, 3 * WH:4 * WH],
                op0=ALU.mult, op1=ALU.subtract))
            P.emit("vector", lambda e: e.tensor_scalar(
                gnr[:, WH:2 * WH], gnr[:, WH:2 * WH], GN_EPS, None, op0=ALU.add))
            tg5 = P.emit("vector", lambda e: e.reciprocal(gnr[:, 3 * WH:4 * WH], gnr[:, WH:2 * WH]))
            tg6 = P.emit("scalar", lambda e: e.activation(gnr[:, WH:2 * WH], gnr[:, 3 * WH:4 * WH], AF.Sqrt),
                         deps=[tg5])
            tzt = None
            for h in range(WH):
                tzt = P.emit("vector", lambda e, hh=h, o=otb: e.tensor_scalar(
                    zt[:, hh * 64:(hh + 1) * 64], o[:, hh * 64:(hh + 1) * 64],
                    gnr[:, 2 * WH + hh:2 * WH + hh + 1], gnr[:, WH + hh:WH + hh + 1],
                    op0=ALU.add, op1=ALU.mult),
                    deps=later(tg6, t_zt_pe) if h == 0 else ())
            t_gn_act = tzt
            t_ot_r[ck % 2] = tzt
            P.emit("vector", lambda e: e.tensor_tensor(zt[:], zt[:], lnw_b[:, 0:C], op=ALU.mult),
                   deps=[tlnb])
            P.emit("vector", lambda e: e.tensor_tensor(zt[:], zt[:], lnw_b[:, C:2 * C], op=ALU.add))
            tz3 = P.emit("vector", lambda e, gg=g1: e.tensor_tensor(zt[:], zt[:], gg[:], op=ALU.mult), deps=[tld_g])
            t_g_r[ck % 2] = tz3
            for kc in range(NCH):
                ttr2 = P.emit("tensor", lambda e, k=kc: e.transpose(
                    p_tra[:, 64:192], zt[:, k * 128:(k + 1) * 128], ident[:]),
                    deps=later(tz3, t_tra_free, t_zf_pe if (ccc, kc) == (0, 0) else None))
                tzf_done = P.emit("vector", lambda e, k=kc, oo=ccc: e.tensor_copy(
                    zf[:, k * 512 + oo * 128: k * 512 + oo * 128 + 128], p_tra[:, 64:192]),
                    deps=[ttr2])
                t_tra_free = tzf_done
                if kc == NCH - 1:
                    t_zt_pe = ttr2

        def wo_group(cg):
            nonlocal t_wob_r, t_xr_r3, t_wo_free, t_xo_r3, t_at_free, t_zf_pe
            t0g = cg * 512
            tmo = None
            for nt in range(NCH):
                bwo = nt % 2
                pw = p_at[0] if bwo == 0 else p_at[1]
                twob_ = P.dma("sync", wob[bwo][:].rearrange("p (kc m) -> p kc m", m=128),
                              boB[:, nt * 128:(nt + 1) * 128].rearrange("(kc p) m -> p kc m", p=128),
                              deps=later(t_wob_r[bwo], tBW[4], *(t_s3b if cg == 0 and nt < 2 else [])),
                              stream="p4wo" + str(bwo))
                txl = P.dma("sync", xrr[bwo][:], xbb[nt * 128:(nt + 1) * 128, t0g:t0g + 512],
                            deps=later(t_xr_r3[bwo], *(t_s3b if cg == 0 and nt < 2 else [])),
                            stream="wvx" + str(bwo))
                for kc in range(NCH):
                    tmo = P.emit("tensor", lambda e, k=kc, b=bwo, pb=pw: e.matmul(
                        pb[:], wob[b][:, k * 128:(k + 1) * 128],
                        zf[:, k * 512:(k + 1) * 512], start=(k == 0), stop=(k == NCH - 1)),
                        deps=later(tzf_done, twob_, t_wo_free[bwo],
                                   t_at_free[bwo]) if kc == 0 else ())
                t_wob_r[bwo] = tmo
                tad = P.emit("vector", lambda e, b=bwo, pb=pw: e.tensor_tensor(
                    xoo[b][:], pb[:], xrr[b][:], op=ALU.add),
                    deps=later(tmo, txl, t_xo_r3[bwo]))
                t_wo_free[bwo] = tad
                t_xr_r3[bwo] = tad
                t_at_free[bwo] = tad
                tso = P.dma("scalar", xbb[nt * 128:(nt + 1) * 128, t0g:t0g + 512], xoo[bwo][:],
                            deps=[tad], stream="wvs" + str(bwo))
                t_xo_r3[bwo] = tso
            t_zf_pe = tmo

        pend = None
        tzf_done = None
        for ck in range(NCKt):
            cc = ck % 4
            tld_r = P.dma("sync", rr1[:], rr_h[ck], deps=later(t_ld_r[0], *t_s3b), stream="wvr")
            tld_k = P.dma("sync", kk1[:], kk_h[ck], deps=later(t_ld_r[1], *t_s3b), stream="wvk")
            tld_e = P.dma("sync", ee1[:], ee_h[ck], deps=later(t_ld_r[2], *t_s3b), stream="wve")
            vv1 = vv1b[ck % 2]
            g1 = g1b[ck % 2]
            otb = otg[ck % 2]
            tld_v = P.dma("sync", vv1[:], vb_h[ck * 128:(ck + 1) * 128, :],
                          deps=later(t_ld_r2[ck % 2], *t_s3b), stream="wvv" + str(ck % 2))
            tld_g = P.dma("sync", g1[:], gg_h[ck * 128:(ck + 1) * 128, :],
                          deps=later(t_g_r[ck % 2], *t_s3b), stream="wvg" + str(ck % 2))
            tsc = None
            for h in range(WH):
                tsc = P.emit("vector", lambda e, hh=h: e.tensor_tensor_scan(
                    scn[:, hh * 128:(hh + 1) * 128], ee1[:, hh * 128:(hh + 1) * 128],
                    zerw[:], 0.0, op0=ALU.add, op1=ALU.bypass),
                    deps=later(tld_e, tz, t_scn_r) if h == 0 else ())
            tsx = P.emit("vector", lambda e: e.tensor_tensor(sxc[:], scn[:], ee1[:], op=ALU.subtract),
                         deps=later(tsc, t_sxc_r))
            tPL = P.emit("scalar", lambda e: e.activation(
                plc[:], scn[:].rearrange("p (h t) -> p h t", t=128)[:, :, 127], AF.Exp, scale=cst[0:WD, 2:3]),
                deps=later(tsc, t_plc_r))
            tQ_ = P.emit("scalar", lambda e: e.activation(scn[:], scn[:], AF.Exp), deps=[tPL, tsx])
            tP_ = P.emit("scalar", lambda e: e.activation(sxc[:], sxc[:], AF.Exp, scale=cst[0:WD, 2:3]), deps=[tsx])
            t_scn_r = tQ_
            t_sxc_r = tP_
            trt = P.emit("vector", lambda e: e.tensor_tensor(rt[:], rr1[:], sxc[:], op=ALU.mult),
                         deps=later(tP_, tld_r, t_rtkt_pe))
            tkt = P.emit("vector", lambda e: e.tensor_tensor(kt_[:], kk1[:], scn[:], op=ALU.mult),
                         deps=later(tQ_, tld_k))
            ttd1 = P.emit("vector", lambda e: e.tensor_tensor(td_[:], rr1[:], kk1[:], op=ALU.mult),
                          deps=later(tld_r, tld_k, tux))
            ttd2 = P.emit("vector", lambda e: e.tensor_tensor(td_[:], td_[:], uxp[:], op=ALU.mult))
            tvda = None
            tev8 = None
            for hg in range(WH // 4):
                ba = hg % 2
                tat = None
                for hi in range(4):
                    h = hg * 4 + hi
                    tat = P.emit("tensor", lambda e, hh=h, b=ba, r4=hi * 128: e.matmul(
                        p_at[b][:, r4:r4 + 128], kt_[:, hh * 128:(hh + 1) * 128], rt[:, hh * 128:(hh + 1) * 128],
                        start=True, stop=True),
                        deps=later(tkt, trt, t_at_free[ba]) if hi == 0 else ())
                tatm = P.emit("vector", lambda e, b=ba: e.tensor_tensor(atm4[:], p_at[b][:], mst4_t[:], op=ALU.mult),
                              deps=later(tat, t_atm_pe))
                t_at_free[ba] = tatm
                for hi in range(4):
                    h = hg * 4 + hi
                    bo = (h // 8) % 2
                    ro = (h % 8) * 64
                    ttdt = P.emit("tensor", lambda e, hh=h: e.transpose(
                        p_tra[:, 0:64], td_[:, hh * 128:(hh + 1) * 128], ident[0:64, 0:64]),
                        deps=later(ttd2, t_tra_free))
                    tred = P.emit("vector", lambda e: e.reduce_sum(ddc[:], p_tra[:, 0:64], axis=AX),
                                  deps=[ttdt])
                    t_tra_free = tred
                    tvd = P.emit("vector", lambda e, hh=h, vv=vv1: e.tensor_scalar(
                        vdd[:], vv[:, hh * 64:(hh + 1) * 64], ddc[:], None, op0=ALU.mult),
                        deps=[tld_v])
                    to1 = P.emit("tensor", lambda e, b=bo, r=ro, hh=h, vv=vv1: e.matmul(
                        p_out[b][:, r:r + 64], atm4[:, (hh % 4) * 128:(hh % 4) * 128 + 128],
                        vv[:, hh * 64:(hh + 1) * 64],
                        start=True, stop=False),
                        deps=later(tatm, tld_v, t_out_evac[bo] if h % 8 == 0 else None))
                    t_atm_pe = to1 if hi == 3 else t_atm_pe
                    to2 = P.emit("tensor", lambda e, b=bo, r=ro, hh=h: e.matmul(
                        p_out[b][:, r:r + 64], rt[:, hh * 128:(hh + 1) * 128],
                        Sbf[:, hh * 64:(hh + 1) * 64], start=False, stop=True), deps=[tSbf])
                    tvda = P.emit("vector", lambda e, b=bo, r=ro: e.tensor_tensor(
                        p_out[b][:, r:r + 64], p_out[b][:, r:r + 64], vdd[:], op=ALU.add),
                        deps=later(to2, tvd))
                    if h % 8 == 7:
                        seg = h // 8
                        tev8 = P.emit("vector", lambda e, b=(h // 8) % 2, s=seg, o=otb: e.tensor_copy(
                            o[:, s * 512:(s + 1) * 512], p_out[b][:]),
                            deps=later(tvda, t_ot_r[ck % 2] if h == 7 else None))
                        t_out_evac[(h // 8) % 2] = tev8
                    tktt = P.emit("tensor", lambda e, hh=h: e.transpose(
                        p_trb[:, 0:64], kt_[:, hh * 128:(hh + 1) * 128], ident[0:64, 0:64]),
                        deps=later(tkt, t_trb_free))
                    tktr = P.emit("vector", lambda e: e.tensor_copy(ktr[:], p_trb[:, 0:64]), deps=[tktt])
                    t_trb_free = tktr
                    tkv = P.emit("tensor", lambda e, hh=h, b=ba, vv=vv1: e.matmul(
                        p_mis[b][0:64, 0:64], ktr[:], vv[:, hh * 64:(hh + 1) * 64],
                        start=True, stop=True), deps=later(tktr, tld_v, t_mis_free[ba]))
                    ts1_ = P.emit("vector", lambda e, hh=h, b=ba: e.tensor_tensor(
                        Sst[:, hh * 64:(hh + 1) * 64], Sst[:, hh * 64:(hh + 1) * 64], p_mis[b][0:64, 0:64],
                        op=ALU.add), deps=[tkv])
                    t_mis_free[ba] = ts1_
                    ts2_ = P.emit("vector", lambda e, hh=h: e.tensor_scalar(
                        Sst[:, hh * 64:(hh + 1) * 64], Sst[:, hh * 64:(hh + 1) * 64], plc[:, hh:hh + 1], None,
                        op0=ALU.mult), deps=[tPL])
                    t_plc_r = ts2_
                    t_rtkt_pe = tkv
                    t_Sbf_pe = to2
            tSbf = P.emit("vector", lambda e: e.tensor_copy(Sbf[:], Sst[:]), deps=later(t_Sbf_pe))
            t_ld_r = [ttd1, tktt, tsx]
            t_ld_r2[ck % 2] = ts1_
            rec = (ck, tld_g, g1, otb, tev8, cc)
            if pend is not None:
                stage2(pend)
                if pend[0] % 4 == 3:
                    wo_group(pend[0] // 4)
            pend = rec
        stage2(pend)
        wo_group(pend[0] // 4)
    t_p4_done = P.barrier()
    if stop_after == "p4":
        tdmp = P.dma("gpsimd", out, xbb, deps=[(k, P.cnt[k]) for k in P.cnt], stream="dump")
        P.emit("scalar", lambda e: e.nop(), deps=[tdmp])
        P.build(); glob.close(); return nc

    ffn_phase(xbb, out, "f2", t_p4_done)

    fin = [(k, P.cnt[k]) for k in P.cnt]
    P.emit("scalar", lambda e: e.nop(), deps=fin)

    P.build()
    glob.close()
    return nc


def _prep_inputs(ins):
    f = np.float32

    def T_(a):
        return np.ascontiguousarray(np.asarray(a, dtype=f).T)

    def u8(a):
        v = np.ascontiguousarray(a).view(np.uint8)[:, 0::4]
        return np.ascontiguousarray(v)

    b = np.arange(8192)
    perm = np.empty(MLP, dtype=np.int64)
    perm[(b // 128) * 256 + (b % 128)] = 2 * b
    perm[(b // 128) * 256 + 128 + (b % 128)] = 2 * b + 1

    maa = np.stack([np.asarray(ins["L0_attn_b_time_maa_" + k], f).reshape(C) for k in
                    ["x", "w", "k", "v", "r", "g"]] + [np.asarray(ins["L0_attn_b_time_decay"], f).reshape(C)])
    maa_p = np.ascontiguousarray(maa.reshape(7, NCH, 128).transpose(2, 0, 1).reshape(128, 7 * NCH))
    lnwb = np.stack([np.asarray(ins["L0_attn_b_ln_w"], f).reshape(C), np.asarray(ins["L0_attn_b_ln_b"], f).reshape(C)])

    common = {
        "wqT": T_(ins["L0_attn_a_Wq"]), "wkT": T_(ins["L0_attn_a_Wk"]),
        "wvT": T_(ins["L0_attn_a_Wv"]), "woT": T_(ins["L0_attn_a_Wo"]),
        "brT": T_(ins["L0_attn_b_W_r"]), "bkT": T_(ins["L0_attn_b_W_k"]),
        "bvT": T_(ins["L0_attn_b_W_v"]), "bgT": T_(ins["L0_attn_b_W_g"]),
        "boT": T_(ins["L0_attn_b_W_o"]),
        "maa": maa_p, "lnwb": np.ascontiguousarray(lnwb),
        "w1": np.ascontiguousarray(np.asarray(ins["L0_attn_b_time_maa_w1"], f)),
        "w2": np.ascontiguousarray(np.asarray(ins["L0_attn_b_time_maa_w2"], f)),
        "dw1": np.ascontiguousarray(np.asarray(ins["L0_attn_b_time_decay_w1"], f)),
        "dw2": np.ascontiguousarray(np.asarray(ins["L0_attn_b_time_decay_w2"], f)),
        "uT": T_(ins["L0_attn_b_time_faaaa"]),
        "gpk": np.ascontiguousarray(u8(ins["L0_ffn_gate_W_packed"]).T[:, perm]),
        "gsc": np.ascontiguousarray(np.asarray(ins["L0_ffn_gate_scales"], f).T[:, perm]),
        "upk": np.ascontiguousarray(u8(ins["L0_ffn_up_W_packed"]).T[:, perm]),
        "usc": np.ascontiguousarray(np.asarray(ins["L0_ffn_up_scales"], f).T[:, perm]),
        "dpk": np.ascontiguousarray(u8(ins["L0_ffn_down_W_packed"]).T),
        "dsc": np.ascontiguousarray(np.asarray(ins["L0_ffn_down_scales"], f).T),
    }
    maps = []
    for bi in range(ins["L0_attn_a_x"].shape[0]):
        m = dict(common)
        m["x"] = np.ascontiguousarray(np.asarray(ins["L0_attn_a_x"][bi], f).T)
        maps.append(m)
    return maps


def kernel(**ins):
    maps = _prep_inputs(ins)
    nc = build_program(T=4096)
    res = run_bass_kernel_spmd(nc, maps, core_ids=list(range(len(maps))))
    return np.stack([np.asarray(r["out"], dtype=np.float32).T for r in res.results])


# version 11
# speedup vs baseline: 1.0600x; 1.0600x over previous
import contextlib
import numpy as np
import ml_dtypes
import concourse.bass as bass
import concourse.mybir as mybir
from concourse.bass_utils import run_bass_kernel_spmd

F32 = mybir.dt.float32
BF16 = mybir.dt.bfloat16
U8 = mybir.dt.uint8
AF = mybir.ActivationFunctionType
ALU = mybir.AluOpType
AX = mybir.AxisListType.X
bfdt = ml_dtypes.bfloat16

C = 2048
NCH = 16
MLP = 16384
SINK = 4
WH = 32
WD = 64
NEG = -30000.0
EXPB = -12.0
LN_EPS = 1e-5
GN_EPS = 1e-5 * 64


class Prog:
    ENG = ["sync", "gpsimd", "vector", "scalar", "tensor"]
    CHAINED = {"vector", "scalar", "gpsimd"}

    def __init__(self, nc):
        self.nc = nc
        self.sem = {}
        self.cnt = {}
        self.seen = {e: {} for e in self.ENG}
        self.ops = {e: [] for e in self.ENG}
        self.lastop = {e: None for e in self.ENG}
        for e in self.ENG:
            self._mk("E" + e)

    def _mk(self, key):
        if key not in self.sem:
            self.sem[key] = self.nc.alloc_semaphore("sm_" + key)
            self.cnt[key] = 0
        return key

    def emit(self, engine, fn, deps=()):
        key = "E" + engine
        waits = {}
        alldeps = list(deps)
        if engine in self.CHAINED and self.lastop[engine] is not None:
            alldeps.append(self.lastop[engine])
        for d in alldeps:
            if d is None:
                continue
            k, n = d
            if n > self.seen[engine].get(k, 0) and n > waits.get(k, 0):
                waits[k] = n
        for k, n in waits.items():
            self.seen[engine][k] = n
        self.cnt[key] += 1
        tok = (key, self.cnt[key])
        self.ops[engine].append((waits, fn, key, 1))
        if engine in self.CHAINED:
            self.lastop[engine] = tok
        return tok

    def dma(self, engine, out, in_, deps=(), stream="misc"):
        key = self._mk("D" + engine[0] + ":" + stream)
        waits = {}
        for d in deps:
            if d is None:
                continue
            k, n = d
            if n > self.seen[engine].get(k, 0) and n > waits.get(k, 0):
                waits[k] = n
        for k, n in waits.items():
            self.seen[engine][k] = n
        self.cnt[key] += 16
        tok = (key, self.cnt[key])
        self.ops[engine].append((waits, lambda e, o=out, i=in_: e.dma_start(out=o, in_=i), key, 16))
        return tok

    def barrier(self):
        toks = [(k, self.cnt[k]) for k in self.cnt]
        outs = []
        for e in self.ENG:
            outs.append(self.emit(e, lambda eng: eng.nop(), deps=toks))
        return outs

    def build(self):
        with self.nc.Block() as block:
            def mk(e):
                def body(eng):
                    for waits, fn, key, inc in self.ops[e]:
                        for k in sorted(waits):
                            eng.wait_ge(self.sem[k], waits[k])
                        fn(eng).then_inc(self.sem[key], inc)
                return body
            block.sync(mk("sync"))
            block.gpsimd(mk("gpsimd"))
            block.vector(mk("vector"))
            block.scalar(mk("scalar"))
            block.tensor(mk("tensor"))


def later(*toks):
    return [t for t in toks if t is not None]


def build_program(T=4096, stop_after='all'):
    nc = bass.Bass("TRN2", target_bir_lowering=False)
    NTB = T // 512
    NCK = T // 128

    def din(name, shape, dt=F32):
        return nc.dram_tensor(name, shape, dt, kind="ExternalInput").ap()

    x_in = din("x", [C, T])
    wqT = din("wqT", [C, C]); wkT = din("wkT", [C, 512]); wvT = din("wvT", [C, 512]); woT = din("woT", [C, C])
    brT = din("brT", [C, C]); bkT = din("bkT", [C, C]); bvT = din("bvT", [C, C])
    bgT = din("bgT", [C, C]); boT = din("boT", [C, C])
    maa = din("maa", [128, 7 * NCH])
    lnwb = din("lnwb", [2, C])
    w1 = din("w1", [C, 160]); w2 = din("w2", [5, 32, C])
    dw1 = din("dw1", [C, 64]); dw2 = din("dw2", [64, C])
    uT = din("uT", [WD, WH])
    gpk = din("gpk", [1024, MLP], U8); gsc = din("gsc", [64, MLP])
    upk = din("upk", [1024, MLP], U8); usc = din("usc", [64, MLP])
    dpk = din("dpk", [8192, C], U8); dsc = din("dsc", [512, C])
    out = nc.dram_tensor("out", [C, T], F32, kind="ExternalOutput").ap()

    def scrt(name, shape, dt):
        return nc.dram_tensor(name, shape, dt, kind="Internal").ap()

    xbb = scrt("xb", [C, T], F32)
    qh = scrt("qh", [NCH, 128, T], BF16)
    kh = scrt("kh", [4, 128, T], BF16)
    vh = scrt("vh", [T, 512], BF16)
    wgq = scrt("wgq", [C, MLP], BF16)
    wuq = scrt("wuq", [C, MLP], BF16)
    wdq = scrt("wdq", [NCH, 128, MLP], BF16)
    inter = scrt("inter", [MLP, T], BF16)
    xnh = scrt("xnh", [C, T], BF16)
    rr_h = scrt("rr_h", [NCK, WD, WH * 128], BF16)
    kk_h = scrt("kk_h", [NCK, WD, WH * 128], BF16)
    ee_h = scrt("ee_h", [NCK, WD, WH * 128], F32)
    vb_h = scrt("vb_h", [T, C], BF16)
    gg_h = scrt("gg_h", [T, C], BF16)
    brB = scrt("brB", [C, C], BF16)
    bkB = scrt("bkB", [C, C], BF16)
    bvB = scrt("bvB", [C, C], BF16)
    bgB = scrt("bgB", [C, C], BF16)
    boB = scrt("boB", [C, C], BF16)

    jj, ii = np.meshgrid(np.arange(128), np.arange(128), indexing="ij")
    ident_c = nc.inline_tensor(np.eye(128, dtype=bfdt), "ident_c").ap()
    m_causal = nc.inline_tensor(np.where(jj <= ii, 0.0, NEG).astype(np.float32), "m_causal").ap()
    m_band = nc.inline_tensor(np.where(jj > ii, 0.0, NEG).astype(np.float32), "m_band").ap()
    m_bandsink = nc.inline_tensor(np.where((jj > ii) | (jj < SINK), 0.0, NEG).astype(np.float32), "m_bandsink").ap()
    m_strict = nc.inline_tensor(np.where(jj < ii, 1.0, 0.0).astype(np.float32), "m_strict").ap()
    m_strict4 = nc.inline_tensor(np.tile(np.where(jj < ii, 1.0, 0.0).astype(np.float32), (1, 4)), "m_strict4").ap()

    P = Prog(nc)
    sb = nc.sbuf_tensor
    ps = nc.psum_tensor
    glob = contextlib.ExitStack()

    ident = glob.enter_context(sb("ident", [128, 128], BF16))
    mc_t = glob.enter_context(sb("mc_t", [128, 128], F32))
    mb_t = glob.enter_context(sb("mb_t", [128, 128], F32))
    mbs_t = glob.enter_context(sb("mbs_t", [128, 128], F32))
    mst_t = glob.enter_context(sb("mst_t", [128, 128], F32))
    mst4_t = glob.enter_context(sb("mst4_t", [128, 512], F32))
    onesc = glob.enter_context(sb("onesc", [128, 1], BF16))
    ones1 = glob.enter_context(sb("ones1", [1, 128], BF16))
    maa_s = glob.enter_context(sb("maa_s", [128, 7 * NCH], F32))
    cst = glob.enter_context(sb("cst", [128, 4], F32))
    tCONST = [
        P.emit("vector", lambda e: e.memset(cst[:, 0:1], 7.5)),
        P.emit("vector", lambda e: e.memset(cst[:, 1:2], EXPB)),
        P.emit("vector", lambda e: e.memset(cst[:, 2:3], -1.0)),
        P.emit("vector", lambda e: e.memset(cst[:, 3:4], 0.08838834764831845)),
        P.dma("sync", ident[:], ident_c, stream="const"),
        P.dma("sync", mc_t[:], m_causal, stream="const"),
        P.dma("sync", mb_t[:], m_band, stream="const"),
        P.dma("sync", mbs_t[:], m_bandsink, stream="const"),
        P.dma("sync", mst_t[:], m_strict, stream="const"),
        P.dma("sync", mst4_t[:], m_strict4, stream="const"),
        P.dma("sync", maa_s[:], maa, stream="const"),
        P.emit("vector", lambda e: e.memset(onesc[:], 1.0)),
        P.emit("vector", lambda e: e.memset(ones1[:], 1.0)),
    ]

    def cast_load(dst_ap, src_ap, deps=(), stream="wcast"):
        return P.dma("gpsimd", dst_ap, src_ap, deps=deps, stream=stream)

    tBW = [P.dma("gpsimd", d, s, stream="wprep") for d, s in
           [(brB, brT), (bkB, bkT), (bvB, bvT), (bgB, bgT), (boB, boT)]]

    def ln_pass(xsrc, tag, pre, consume):
        with contextlib.ExitStack() as ctx:
            xbf = [ctx.enter_context(sb(f"{tag}xb{i}", [128, NCH * 512], BF16)) for i in range(2)]
            sqb = ctx.enter_context(sb(f"{tag}sq", [128, NCH * 512], BF16))
            rowf = ctx.enter_context(sb(f"{tag}rf", [1, 4 * 512], F32))
            rowb = ctx.enter_context(sb(f"{tag}rb", [1, 2 * 512], BF16))
            p_bc = ctx.enter_context(ps(f"{tag}pb", [128, 1024], F32))
            p_st = p_bc[0:1, :]
            t_x_r = [list(pre), list(pre)]
            t_bcfree = None
            t_stfree = None
            for tb in range(NTB):
                t0 = tb * 512
                b = tb % 2
                src = xsrc[:, t0:t0 + 512].rearrange("(kc p) t -> p kc t", p=128)
                tld = P.dma("gpsimd", xbf[b][:].rearrange("p (kc t) -> p kc t", t=512), src,
                            deps=later(*t_x_r[b], *tCONST), stream="lnx" + str(b))
                tsq = P.emit("scalar", lambda e, bb=b: e.activation(sqb[:], xbf[bb][:], AF.Square), deps=[tld])
                for kc in range(NCH):
                    tmm = P.emit("tensor", lambda e, k=kc, bb=b: e.matmul(
                        p_st[:, 0:512], onesc[:], xbf[bb][:, k * 512:(k + 1) * 512],
                        start=(k == 0), stop=(k == NCH - 1)),
                        deps=later(tld, t_stfree, t_bcfree, *tCONST) if kc == 0 else ())
                for kc in range(NCH):
                    tmm = P.emit("tensor", lambda e, k=kc: e.matmul(
                        p_st[:, 512:1024], onesc[:], sqb[:, k * 512:(k + 1) * 512],
                        start=(k == 0), stop=(k == NCH - 1)),
                        deps=[tsq] if kc == 0 else ())
                r1 = P.emit("vector", lambda e: e.tensor_scalar(rowf[:, 0:512], p_st[:, 0:512], -1.0 / C, None, op0=ALU.mult), deps=[tmm])
                P.emit("vector", lambda e: e.tensor_tensor(rowf[:, 512:1024], rowf[:, 0:512], rowf[:, 0:512], op=ALU.mult))
                r3 = P.emit("vector", lambda e: e.scalar_tensor_tensor(rowf[:, 1024:1536], p_st[:, 512:1024], 1.0 / C,
                                                                      rowf[:, 512:1024], op0=ALU.mult, op1=ALU.subtract))
                t_stfree = r3
                P.emit("vector", lambda e: e.tensor_scalar(rowf[:, 1024:1536], rowf[:, 1024:1536], LN_EPS, None, op0=ALU.add))
                r5 = P.emit("vector", lambda e: e.reciprocal(rowf[:, 1536:2048], rowf[:, 1024:1536]))
                r6 = P.emit("scalar", lambda e: e.activation(rowb[:, 512:1024], rowf[:, 1536:2048], AF.Sqrt), deps=[r5])
                r7 = P.emit("vector", lambda e: e.tensor_copy(rowb[:, 0:512], rowf[:, 0:512]))
                b1 = P.emit("tensor", lambda e: e.matmul(p_bc[:, 0:512], ones1[:], rowb[:, 0:512], start=True, stop=True),
                            deps=later(r7, t_bcfree))
                b2 = P.emit("tensor", lambda e: e.matmul(p_bc[:, 512:1024], ones1[:], rowb[:, 512:1024], start=True, stop=True),
                            deps=[r6])
                tlast = None
                for kc in range(NCH):
                    P.emit("vector", lambda e, k=kc, bb=b: e.tensor_tensor(
                        xbf[bb][:, k * 512:(k + 1) * 512], xbf[bb][:, k * 512:(k + 1) * 512], p_bc[:, 0:512], op=ALU.add),
                        deps=later(b1, b2, tsq) if kc == 0 else ())
                    tlast = P.emit("vector", lambda e, k=kc, bb=b: e.tensor_tensor(
                        xbf[bb][:, k * 512:(k + 1) * 512], xbf[bb][:, k * 512:(k + 1) * 512], p_bc[:, 512:1024], op=ALU.mult))
                t_bcfree = tlast
                rd = consume(tb, xbf[b], tlast)
                t_x_r[b] = rd if isinstance(rd, list) else [rd]

    ctx0 = contextlib.ExitStack()
    NS = 1024
    dq_pk = [ctx0.enter_context(sb(f"dqpk{i}", [128, NS], U8)) for i in range(2)]
    dq_sgb = [ctx0.enter_context(sb(f"dqsg{i}", [128, NS], BF16)) for i in range(2)]
    dq_c8 = [ctx0.enter_context(sb(f"dqc8{i}", [128, NS], U8)) for i in range(2)]
    dq_cbf = [ctx0.enter_context(sb(f"dqcbf{i}", [128, NS], BF16)) for i in range(2)]
    dq_sgn = [ctx0.enter_context(sb(f"dqsgn{i}", [128, NS], BF16)) for i in range(2)]
    dq_mcb = ctx0.enter_context(sb("dqmc", [128, NS], BF16))
    dq_b2b = ctx0.enter_context(sb("dqb2", [128, NS], BF16))
    dq_m1b = ctx0.enter_context(sb("dqm1", [128, NS], BF16))
    dq_rrb = ctx0.enter_context(sb("dqrr", [128, NS], BF16))
    dq_wo = [ctx0.enter_context(sb(f"dqwo{i}", [128, NS], BF16)) for i in range(4)]

    dq_jobs = []
    for (pkap, scap, kind, nrb, nsl) in [
            (gpk, gsc, "g", 8, 16), (upk, usc, "u", 8, 16), (dpk, dsc, "d", 64, 2)]:
        for rb in range(nrb):
            for nsi in range(nsl):
                dq_jobs.append((pkap, scap, kind, rb, nsi))
    dq_st = {"pk": [None, None], "sg": [None, None], "cbf": [None, None],
             "sgn": [None, None], "wo": [None] * 4, "j": 0}

    def dq_pump(n=1):
        for _ in range(n):
            j = dq_st["j"]
            if j >= len(dq_jobs):
                return False
            dq_st["j"] = j + 1
            pkap, scap, kind, rb, nsi = dq_jobs[j]
            bj = j % 2
            tpk = P.dma("sync", dq_pk[bj][:], pkap[rb * 128:(rb + 1) * 128, nsi * NS:(nsi + 1) * NS],
                        deps=later(dq_st["pk"][bj]), stream="dqp" + str(bj))
            ssrc = scap[rb * 8:(rb + 1) * 8, nsi * NS:(nsi + 1) * NS].unsqueeze(1).broadcast_to([8, 16, NS])
            tsg = P.dma("gpsimd", dq_sgb[bj][:], ssrc, deps=later(dq_st["sg"][bj]), stream="dqs" + str(bj))
            t80 = P.emit("vector", lambda e, b=bj: e.tensor_scalar(
                dq_c8[0][:].bitcast(mybir.dt.uint32), dq_pk[b][:].bitcast(mybir.dt.uint32),
                0x0F0F0F0F, None, op0=ALU.bitwise_and), deps=[tpk])
            t81 = P.emit("vector", lambda e, b=bj: e.tensor_scalar(
                dq_c8[1][:].bitcast(mybir.dt.uint32), dq_pk[b][:].bitcast(mybir.dt.uint32),
                4, 0x0F0F0F0F, op0=ALU.logical_shift_right, op1=ALU.bitwise_and), deps=[tpk])
            dq_st["pk"][bj] = t81
            tcbs = []
            for nib in range(2):
                tcbs.append(P.emit("vector", lambda e, n=nib: e.tensor_copy(dq_cbf[n][:], dq_c8[n][:]),
                                   deps=[t80 if nib == 0 else t81, dq_st["cbf"][nib]]))
            tsns = []
            for nib in range(2):
                tsns.append(P.emit("scalar", lambda e, n=nib: e.activation(
                    dq_sgn[n][:], dq_cbf[n][:], AF.Sign, bias=cst[:, 0:1], scale=cst[:, 2:3]),
                    deps=[tcbs[nib], dq_st["sgn"][nib]]))
            two = None
            for nib in range(2):
                tm4 = P.emit("vector", lambda e, n=nib: e.scalar_tensor_tensor(
                    dq_mcb[:], dq_sgn[n][:], 4.0, dq_cbf[n][:], op0=ALU.mult, op1=ALU.add),
                    deps=[tsns[nib]])
                P.emit("vector", lambda e: e.tensor_scalar(dq_mcb[:], dq_mcb[:], -4.0, None, op0=ALU.add))
                P.emit("vector", lambda e: e.tensor_scalar(dq_b2b[:], dq_mcb[:], -2.0, None, op0=ALU.add))
                P.emit("vector", lambda e: e.tensor_scalar(dq_rrb[:], dq_mcb[:], -6.0, 0.0,
                                                           op0=ALU.add, op1=ALU.max))
                P.emit("vector", lambda e: e.scalar_tensor_tensor(dq_m1b[:], dq_mcb[:], 0.5, dq_b2b[:],
                                                                 op0=ALU.mult, op1=ALU.max))
                P.emit("vector", lambda e: e.tensor_tensor(dq_m1b[:], dq_m1b[:], dq_rrb[:], op=ALU.add))
                P.emit("vector", lambda e, n=nib: e.tensor_tensor(dq_sgn[n][:], dq_m1b[:], dq_sgn[n][:], op=ALU.mult))
                dq_st["cbf"][nib] = tm4
                wsl = bj * 2 + nib
                two = P.emit("vector", lambda e, b=wsl, n=nib, s=bj: e.tensor_tensor(
                    dq_wo[b][:], dq_sgn[n][:], dq_sgb[s][:], op=ALU.mult),
                    deps=later(dq_st["wo"][wsl], tsg))
                dq_st["sgn"][nib] = two
                if kind in ("g", "u"):
                    wdst = wgq if kind == "g" else wuq
                    dst = wdst[rb * 256:(rb + 1) * 256, nsi * NS:(nsi + 1) * NS].rearrange(
                        "(p two) m -> two p m", two=2)[nib]
                    tst = P.dma("scalar", dst, dq_wo[wsl][:], deps=[two], stream="dqw" + str(wsl))
                else:
                    mc2 = 2 * rb + nib
                    dst = wdq[nsi * 8:nsi * 8 + 8, :, mc2 * 128:(mc2 + 1) * 128].transpose([1, 0, 2])
                    tst = P.dma("scalar", dst, dq_wo[wsl][:].rearrange("q (hti hc) -> q hti hc", hc=128),
                                deps=[two], stream="dqw" + str(wsl))
                dq_st["wo"][wsl] = tst
            dq_st["sg"][bj] = two
        return True

    dq_pump(6)

    with contextlib.ExitStack() as ctx:
        wq_s = ctx.enter_context(sb("wq_s", [128, NCH * C], BF16))
        wk_s = ctx.enter_context(sb("wk_s", [128, NCH * 512], BF16))
        wv_s = ctx.enter_context(sb("wv_s", [128, NCH * 512], BF16))
        evq = [ctx.enter_context(sb(f"p1evq{i}", [128, 512], BF16)) for i in range(4)]
        evv = [ctx.enter_context(sb(f"p1evv{i}", [128, 512], BF16)) for i in range(2)]
        p_qk = [ctx.enter_context(ps(f"p1pqk{i}", [128, 512], F32)) for i in range(4)]
        p_v = [ctx.enter_context(ps(f"p1pv{i}", [128, 512], F32)) for i in range(2)]
        twl = []
        for kc in range(NCH):
            twl.append(cast_load(wq_s[:, kc * C:(kc + 1) * C], wqT[kc * 128:(kc + 1) * 128, :],
                                 stream="p1w"))
            twl.append(cast_load(wk_s[:, kc * 512:(kc + 1) * 512], wkT[kc * 128:(kc + 1) * 128, :], stream="p1w"))
            twl.append(cast_load(wv_s[:, kc * 512:(kc + 1) * 512], wvT[kc * 128:(kc + 1) * 128, :], stream="p1w"))
        st_p1 = {"qk": [None] * 4, "v": [None] * 2, "eq": [None] * 4, "ev": [None] * 2}

        def consume_p1(tb, xn_t, txn):
            t0 = tb * 512
            tmq = None
            for nt in range(NCH + 4):
                dq_pump(1)
                isq = nt < NCH
                bi = nt % 4
                w_s = wq_s if isq else wk_s
                wof = (nt * 128) if isq else ((nt - NCH) * 128)
                wstr = C if isq else 512
                for kc in range(NCH):
                    tmq = P.emit("tensor", lambda e, k=kc, b=bi, w=w_s, o=wof, s=wstr, xn=xn_t: e.matmul(
                        p_qk[b][:], w[:, k * s + o: k * s + o + 128], xn[:, k * 512:(k + 1) * 512],
                        start=(k == 0), stop=(k == NCH - 1)),
                        deps=later(txn, st_p1["qk"][bi], *(twl if tb == 0 else [])) if kc == 0 else ())
                tev = P.emit("vector", lambda e, b=bi: e.tensor_copy(evq[b][:], p_qk[b][:]),
                             deps=later(tmq, st_p1["eq"][bi]))
                st_p1["qk"][bi] = tev
                dst = qh[nt, :, t0:t0 + 512] if isq else kh[nt - NCH, :, t0:t0 + 512]
                tst = P.dma("scalar", dst, evq[bi][:], deps=[tev], stream="p1qk" + str(bi))
                st_p1["eq"][bi] = tst
            tmv = None
            for tt4 in range(4):
                dq_pump(1)
                bi = tt4 % 2
                for kc in range(NCH):
                    tmv = P.emit("tensor", lambda e, k=kc, s=tt4, b=bi, xn=xn_t: e.matmul(
                        p_v[b][:], xn[:, k * 512 + s * 128: k * 512 + s * 128 + 128], wv_s[:, k * 512:(k + 1) * 512],
                        start=(k == 0), stop=(k == NCH - 1)),
                        deps=later(txn, st_p1["v"][bi]) if kc == 0 else ())
                tev = P.emit("vector", lambda e, b=bi: e.tensor_copy(evv[b][:], p_v[b][:]),
                             deps=later(tmv, st_p1["ev"][bi]))
                st_p1["v"][bi] = tev
                tst = P.dma("scalar", vh[t0 + tt4 * 128: t0 + tt4 * 128 + 128, :], evv[bi][:], deps=[tev],
                            stream="p1v" + str(bi))
                st_p1["ev"][bi] = tst
            return [tmq, tmv]

        ln_pass(x_in, "p1", [], consume_p1)
    t_p1_done = P.barrier()
    if stop_after == "p1":
        while dq_pump():
            pass
        P.emit("scalar", lambda e: e.nop(), deps=[(k, P.cnt[k]) for k in P.cnt])
        P.build(); glob.close(); return nc

    with contextlib.ExitStack() as ctx:
        wo_s = ctx.enter_context(sb("wo_s", [128, NCH * C], BF16))
        qq = ctx.enter_context(sb("qq", [128, NCH * 512], BF16))
        kk = ctx.enter_context(sb("kk", [128, 4 * 12 * 128], BF16))
        vaug = ctx.enter_context(sb("vaug", [128, 12 * 516], BF16))
        es = ctx.enter_context(sb("es", [128, 12 * 512], BF16))
        essk = ctx.enter_context(sb("essk", [4, 512], BF16))
        ksink = ctx.enter_context(sb("ksink", [128, 4 * 128], BF16))
        vsink = ctx.enter_context(sb("vsink", [4, 4 * 129], BF16))
        yf = ctx.enter_context(sb("yf", [128, NCH * 512], BF16))
        ytt = ctx.enter_context(sb("ytt", [128, 128], BF16))
        rcp = ctx.enter_context(sb("rcp", [128, 4], F32))
        xo = [ctx.enter_context(sb(f"p2xo{i}", [128, 512], F32)) for i in range(2)]
        xr = [ctx.enter_context(sb(f"p2xr{i}", [128, 512], F32)) for i in range(2)]
        p_sc = [ctx.enter_context(ps(f"p2sc{i}", [128, 512], F32)) for i in range(2)]
        p_av = [ctx.enter_context(ps(f"p2av{i}", [128, 512], F32)) for i in range(2)]
        p_sk = ctx.enter_context(ps("p2sk", [4, 512], F32))
        p_tr = ctx.enter_context(ps("p2tr", [128, 128], BF16))
        p_wo = [ctx.enter_context(ps(f"p2wo{i}", [128, 512], F32)) for i in range(2)]

        twl = [cast_load(wo_s[:, kc * C:(kc + 1) * C], woT[kc * 128:(kc + 1) * 128, :],
                         deps=(t_p1_done if kc == 0 else ()), stream="p2w") for kc in range(NCH)]
        tva0 = P.emit("vector", lambda e: e.memset(
            vaug[:].rearrange("p (kt h o) -> p kt h o", h=4, o=129)[:, :, :, 128], 1.0))
        tvs0 = P.emit("vector", lambda e: e.memset(
            vsink[:].rearrange("p (h o) -> p h o", o=129)[:, :, 128], 1.0))
        tks = P.dma("sync", ksink[:].rearrange("p (h t) -> p h t", t=128),
                    kh[:, :, 0:128].transpose([1, 0, 2]), deps=t_p1_done, stream="p2snk")
        tvsl = P.dma("sync", vsink[:].rearrange("p (h o) -> p h o", o=129)[:, :, 0:128],
                     vh[0:4, :].rearrange("p (h d) -> p h d", d=128), deps=later(*t_p1_done, tvs0), stream="p2snk")

        t_q_r = None
        t_kk_r = None
        t_va_r = tva0
        t_es_w = {}
        t_es_r = {}
        t_sc_free = [None, None]
        t_av_free = [None, None]
        t_sk_free = None
        t_essk_r = None
        t_tr_free = None
        t_ytt_r = None
        t_rcp_r = None
        t_yf_r = None
        t_yf_w = {}
        t_wo_free = [None, None]
        t_xo_r = [None, None]
        t_xr_r = [None, None]

        for qb in range(NTB):
            q0 = qb * 512
            kt_lo = max(0, qb * 4 - 8)
            nkt = qb * 4 + 4 - kt_lo
            tqld = P.dma("sync", qq[:].rearrange("p (h t) -> p h t", t=512),
                         qh[:, :, q0:q0 + 512].transpose([1, 0, 2]),
                         deps=later(t_q_r, *t_p1_done), stream="p2q")
            tkld = P.dma("sync", kk[:, 0:4 * nkt * 128].rearrange("p (h t) -> p h t", t=nkt * 128),
                         kh[:, :, kt_lo * 128:(kt_lo + nkt) * 128].transpose([1, 0, 2]),
                         deps=later(t_kk_r, *t_p1_done), stream="p2k")
            tvld = None
            for ki2 in range(nkt):
                tvld = P.dma("sync",
                             vaug[:, ki2 * 516:(ki2 + 1) * 516].rearrange("p (h o) -> p h o", o=129)[:, :, 0:128],
                             vh[(kt_lo + ki2) * 128:(kt_lo + ki2 + 1) * 128, :].rearrange("p (h d) -> p h d", d=128),
                             deps=later(t_va_r, *t_p1_done) if ki2 == 0 else (), stream="p2v")
            tevy = None
            for h in range(NCH):
                dq_pump(1)
                kvh = h // 4
                if qb >= 2:
                    tskm = P.emit("tensor", lambda e, kv=kvh, hh=h: e.matmul(
                        p_sk[:], ksink[:, kv * 128:kv * 128 + 4], qq[:, hh * 512:(hh + 1) * 512],
                        start=True, stop=True), deps=later(tks, tqld, t_sk_free))
                    tske = P.emit("scalar", lambda e: e.activation(essk[:], p_sk[:], AF.Exp, bias=cst[0:4, 1:2], scale=cst[0:4, 3:4]),
                                  deps=later(tskm, t_essk_r))
                    t_sk_free = tske
                for ki in range(nkt):
                    kt = kt_lo + ki
                    bsc = ki % 2
                    tmm = P.emit("tensor", lambda e, kv=kvh, i=ki, hh=h, b=bsc, n=nkt: e.matmul(
                        p_sc[b][:], kk[:, (kv * n + i) * 128:(kv * n + i) * 128 + 128],
                        qq[:, hh * 512:(hh + 1) * 512], start=True, stop=True),
                        deps=later(tkld, tqld, t_sc_free[bsc]))
                    tmask = tmm
                    for qs in range(4):
                        qt = qb * 4 + qs
                        if kt == qt:
                            mt = mc_t
                        elif kt == qt - 8:
                            mt = mbs_t if kt == 0 else mb_t
                        else:
                            continue
                        tmask = P.emit("vector", lambda e, b=bsc, q=qs, m=mt: e.tensor_tensor(
                            p_sc[b][:, q * 128:(q + 1) * 128], p_sc[b][:, q * 128:(q + 1) * 128], m[:], op=ALU.add),
                            deps=[tmask])
                    texp = P.emit("scalar", lambda e, b=bsc, i=ki: e.activation(
                        es[:, i * 512:(i + 1) * 512], p_sc[b][:], AF.Exp, bias=cst[:, 1:2], scale=cst[:, 3:4]),
                        deps=later(tmask, t_es_r.get(ki)))
                    t_sc_free[bsc] = texp
                    t_es_w[ki] = texp
                for qs in range(4):
                    qt = qb * 4 + qs
                    bav = qs % 2
                    col = (qs // 2) * 258
                    valid = list(range(max(kt_lo, qt - 8), qt + 1))
                    has_sink = qt >= 9
                    tav = None
                    for vi, kt in enumerate(valid):
                        ki = kt - kt_lo
                        is_last = (vi == len(valid) - 1) and not has_sink
                        tav = P.emit("tensor", lambda e, i=ki, q=qs, kv=kvh, b=bav, cl=col,
                                     stt=(vi == 0), sp=is_last: e.matmul(
                            p_av[b][:, cl:cl + 129], es[:, i * 512 + q * 128: i * 512 + q * 128 + 128],
                            vaug[:, i * 516 + kv * 129: i * 516 + kv * 129 + 129],
                            start=stt, stop=sp),
                            deps=later(t_es_w[ki], tvld, t_av_free[bav]) if vi == 0 else [t_es_w[ki]])
                        t_es_r[ki] = tav
                    if has_sink:
                        tav = P.emit("tensor", lambda e, q=qs, kv=kvh, b=bav, cl=col: e.matmul(
                            p_av[b][:, cl:cl + 129], essk[:, q * 128:(q + 1) * 128],
                            vsink[:, kv * 129:(kv + 1) * 129], start=False, stop=True),
                            deps=later(t_sk_free, tvsl))
                        t_essk_r = tav
                    trc = P.emit("vector", lambda e, b=bav, cl=col, q=qs: e.reciprocal(
                        rcp[:, q:q + 1], p_av[b][:, cl + 128:cl + 129]), deps=later(tav, t_rcp_r))
                    tsc2 = P.emit("scalar", lambda e, b=bav, cl=col, q=qs: e.activation(
                        ytt[:], p_av[b][:, cl:cl + 128], AF.Copy, scale=rcp[:, q:q + 1]),
                        deps=later(trc, t_ytt_r))
                    t_av_free[bav] = tsc2
                    t_rcp_r = tsc2
                    ttr = P.emit("tensor", lambda e: e.transpose(p_tr[:], ytt[:], ident[:]),
                                 deps=later(tsc2, t_tr_free))
                    t_ytt_r = ttr
                    tevy = P.emit("vector", lambda e, hh=h, q=qs: e.tensor_copy(
                        yf[:, hh * 512 + q * 128: hh * 512 + q * 128 + 128], p_tr[:]),
                        deps=later(ttr, t_yf_r if (h, qs) == (0, 0) else None))
                    t_tr_free = tevy
                    t_yf_w[(h, qs)] = tevy
            t_q_r = tevy
            t_kk_r = tevy
            t_va_r = tevy
            yf_deps = list(t_yf_w.values())
            tmo = None
            for nt in range(NCH):
                dq_pump(1)
                bwo = nt % 2
                txl = P.dma("sync", xr[bwo][:], x_in[nt * 128:(nt + 1) * 128, q0:q0 + 512],
                            deps=later(t_xr_r[bwo], *(t_p1_done if qb == 0 and nt < 2 else [])),
                            stream="p2xr" + str(bwo))
                for kc in range(NCH):
                    tmo = P.emit("tensor", lambda e, k=kc, n=nt, b=bwo: e.matmul(
                        p_wo[b][:], wo_s[:, k * C + n * 128: k * C + n * 128 + 128],
                        yf[:, k * 512:(k + 1) * 512], start=(k == 0), stop=(k == NCH - 1)),
                        deps=later(*yf_deps, *(twl if qb == 0 and nt == 0 else []), t_wo_free[bwo]) if kc == 0 else ())
                tad = P.emit("vector", lambda e, b=bwo: e.tensor_tensor(xo[b][:], p_wo[b][:], xr[b][:], op=ALU.add),
                             deps=later(tmo, txl, t_xo_r[bwo]))
                t_wo_free[bwo] = tad
                t_xr_r[bwo] = tad
                tso = P.dma("scalar", xbb[nt * 128:(nt + 1) * 128, q0:q0 + 512], xo[bwo][:], deps=[tad],
                            stream="p2st" + str(bwo))
                t_xo_r[bwo] = tso
            t_yf_r = tmo
            t_yf_w = {}
    while dq_pump():
        pass
    t_p2_done = P.barrier()
    t_dq_done = t_p2_done
    ctx0.close()
    if stop_after == "p2":
        tdmp = P.dma("gpsimd", out, xbb, deps=[(k, P.cnt[k]) for k in P.cnt], stream="dump")
        P.emit("scalar", lambda e: e.nop(), deps=[tdmp])
        P.build(); glob.close(); return nc

    def ffn_phase(xsrc, xdst, tag, pre):
        with contextlib.ExitStack() as ctx:
            with contextlib.ExitStack() as c2:
                xn_all = c2.enter_context(sb(f"{tag}xna", [128, NCH * T], BF16))
                tbds = []

                def consume_ffn(tb, xbuf, tok):
                    tlast = None
                    for kc in range(NCH):
                        tlast = P.emit("vector", lambda e, k=kc, xb_=xbuf, t0=tb * 512: e.tensor_copy(
                            xn_all[:, k * T + t0: k * T + t0 + 512], xb_[:, k * 512:(k + 1) * 512]),
                            deps=[tok] if kc == 0 else ())
                    tbds.append(tlast)
                    return tlast

                ln_pass(xsrc, tag + "l", pre, consume_ffn)
                wg_t = [c2.enter_context(sb(f"{tag}wg{i}", [128, NCH * 128], BF16)) for i in range(2)]
                wu_t = [c2.enter_context(sb(f"{tag}wu{i}", [128, NCH * 128], BF16)) for i in range(2)]
                gb = c2.enter_context(sb(f"{tag}gb", [128, 512], BF16))
                ub = c2.enter_context(sb(f"{tag}ub", [128, 512], BF16))
                geb = c2.enter_context(sb(f"{tag}geb", [128, 512], BF16))
                ib = [c2.enter_context(sb(f"{tag}ib{i}", [128, NTB * 512], BF16)) for i in range(2)]
                p_g = [c2.enter_context(ps(f"{tag}pg{i}", [128, 512], F32)) for i in range(2)]
                p_u = [c2.enter_context(ps(f"{tag}pu{i}", [128, 512], F32)) for i in range(2)]
                t_wg_r = [None, None]
                t_wu_r = [None, None]
                t_pg_free = [None, None]
                t_pu_free = [None, None]
                t_ib_r = [None, None]
                for nt in range(128):
                    bw = nt % 2
                    gate0 = (list(t_dq_done) + list(pre) + [tbds[-1]]) if nt < 2 else []
                    twg = P.dma("sync", wg_t[bw][:].rearrange("p (kc m) -> p kc m", m=128),
                                wgq[:, nt * 128:(nt + 1) * 128].rearrange("(kc p) m -> p kc m", p=128),
                                deps=later(t_wg_r[bw], *gate0), stream="ffwg" + str(bw))
                    twu = P.dma("sync", wu_t[bw][:].rearrange("p (kc m) -> p kc m", m=128),
                                wuq[:, nt * 128:(nt + 1) * 128].rearrange("(kc p) m -> p kc m", p=128),
                                deps=later(t_wu_r[bw], *gate0), stream="ffwu" + str(bw))
                    tibd = None
                    for tb in range(NTB):
                        bp = tb % 2
                        tg = tu = None
                        for kc in range(NCH):
                            tg = P.emit("tensor", lambda e, k=kc, b=bw, p=bp, t0=tb * 512: e.matmul(
                                p_g[p][:], wg_t[b][:, k * 128:(k + 1) * 128],
                                xn_all[:, k * T + t0: k * T + t0 + 512],
                                start=(k == 0), stop=(k == NCH - 1)),
                                deps=later(twg, tbds[tb], t_pg_free[bp]) if kc == 0 else ())
                        for kc in range(NCH):
                            tu = P.emit("tensor", lambda e, k=kc, b=bw, p=bp, t0=tb * 512: e.matmul(
                                p_u[p][:], wu_t[b][:, k * 128:(k + 1) * 128],
                                xn_all[:, k * T + t0: k * T + t0 + 512],
                                start=(k == 0), stop=(k == NCH - 1)),
                                deps=later(twu, tbds[tb], t_pu_free[bp]) if kc == 0 else ())
                        tgb = P.emit("vector", lambda e, p=bp: e.tensor_copy(gb[:], p_g[p][:]), deps=[tg])
                        t_pg_free[bp] = tgb
                        tge = P.emit("scalar", lambda e: e.activation(geb[:], gb[:], AF.Gelu_apprx_tanh),
                                     deps=[tgb])
                        tub = P.emit("vector", lambda e, p=bp: e.tensor_copy(ub[:], p_u[p][:]), deps=[tu])
                        t_pu_free[bp] = tub
                        tibd = P.emit("vector", lambda e, b=bw, t=tb: e.tensor_tensor(
                            ib[b][:, t * 512:(t + 1) * 512], geb[:], ub[:], op=ALU.mult),
                            deps=later(tge, t_ib_r[bw] if tb == 0 else None))
                    tist = P.dma("scalar", inter[nt * 128:(nt + 1) * 128, :], ib[bw][:], deps=[tibd],
                                 stream="ffis" + str(bw))
                    t_ib_r[bw] = tist
                    t_wg_r[bw] = tg
                    t_wu_r[bw] = tu
            t_s2a = P.barrier()
            with contextlib.ExitStack() as c2:
                it_all = c2.enter_context(sb(f"{tag}ita", [128, 128 * 512], BF16))
                wd_t = c2.enter_context(sb(f"{tag}wd", [128, MLP], BF16))
                xrr = [c2.enter_context(sb(f"{tag}xr{i}", [128, 512], F32)) for i in range(2)]
                xoo = [c2.enter_context(sb(f"{tag}xo{i}", [128, 512], F32)) for i in range(2)]
                p_d = [c2.enter_context(ps(f"{tag}pd{i}", [128, 512], F32)) for i in range(2)]
                t_it_r = None
                t_wd_r = [None, None]
                t_pd_free = [None, None]
                t_xr_r2 = [None, None]
                t_xo_r2 = [None, None]
                for tb in range(NTB):
                    tits = []
                    for g8 in range(8):
                        tits.append(P.dma("sync",
                                          it_all[:, g8 * 16 * 512:(g8 + 1) * 16 * 512].rearrange("p (m t) -> p m t", t=512),
                                          inter[g8 * C:(g8 + 1) * C, tb * 512:(tb + 1) * 512].rearrange("(m p) t -> p m t", p=128),
                                          deps=later(t_it_r, *t_s2a), stream="ffit"))
                    for hti in range(NCH):
                        bp = hti % 2
                        gate1 = (list(t_dq_done) + list(t_s2a)) if tb == 0 and hti == 0 else []
                        tw0 = P.dma("sync", wd_t[:, 0:8192], wdq[hti][:, 0:8192],
                                    deps=later(t_wd_r[0], *gate1), stream="ffwdA")
                        tw1 = P.dma("sync", wd_t[:, 8192:MLP], wdq[hti][:, 8192:MLP],
                                    deps=later(t_wd_r[1], *gate1), stream="ffwdB")
                        tdm = None
                        for mc2 in range(128):
                            half_tok = tw1 if mc2 == 64 else None
                            tdm = P.emit("tensor", lambda e, m=mc2, p=bp: e.matmul(
                                p_d[p][:], wd_t[:, m * 128:(m + 1) * 128], it_all[:, m * 512:(m + 1) * 512],
                                start=(m == 0), stop=(m == 127)),
                                deps=later(tw0, *tits, t_pd_free[bp]) if mc2 == 0 else later(half_tok))
                            if mc2 == 63:
                                t_wd_r[0] = tdm
                            if mc2 == 127:
                                t_wd_r[1] = tdm
                        txl = P.dma("sync", xrr[bp][:], xsrc[hti * 128:(hti + 1) * 128, tb * 512:(tb + 1) * 512],
                                    deps=later(t_xr_r2[bp], *(t_s2a if tb == 0 and hti < 2 else [])),
                                    stream="ffxr" + str(bp))
                        tad = P.emit("vector", lambda e, b=bp: e.tensor_tensor(
                            xoo[b][:], p_d[b][:], xrr[b][:], op=ALU.add),
                            deps=later(tdm, txl, t_xo_r2[bp]))
                        t_pd_free[bp] = tad
                        t_xr_r2[bp] = tad
                        tso = P.dma("scalar", xdst[hti * 128:(hti + 1) * 128, tb * 512:(tb + 1) * 512], xoo[bp][:],
                                    deps=[tad], stream="ffst" + str(bp))
                        t_xo_r2[bp] = tso
                    t_it_r = tdm
        return P.barrier()

    t_p3_done = ffn_phase(xbb, xbb, "f1", t_p2_done)
    if stop_after == "p3":
        tdmp = P.dma("gpsimd", out, xbb, deps=[(k, P.cnt[k]) for k in P.cnt], stream="dump")
        P.emit("scalar", lambda e: e.nop(), deps=[tdmp])
        P.build(); glob.close(); return nc

    with contextlib.ExitStack() as ctx:
        xst = ctx.enter_context(sb("p4xst", [128, NCH * 512], BF16))
        st4 = {"r": None}

        def consume_p4(tb, xbuf, tok):
            t0 = tb * 512
            tcp = P.emit("vector", lambda e, xb_=xbuf: e.tensor_copy(xst[:], xb_[:]),
                         deps=later(tok, st4["r"]))
            tst = P.dma("scalar", xnh[:, t0:t0 + 512].rearrange("(kc p) t -> p kc t", p=128),
                        xst[:].rearrange("p (kc t) -> p kc t", t=512), deps=[tcp], stream="p4xnh")
            st4["r"] = tst
            return tcp

        ln_pass(xbb, "p4l", t_p3_done, consume_p4)
    t_ln4 = P.barrier()

    with contextlib.ExitStack() as ctx:
        lora1 = ctx.enter_context(sb("lora1", [32, 5 * T], BF16))
        w1_s = ctx.enter_context(sb("w1_s", [128, NCH * 160], BF16))
        w2_s = ctx.enter_context(sb("w2_s", [32, 5 * C], BF16))
        dw1_s = ctx.enter_context(sb("dw1_s", [128, NCH * 64], BF16))
        dw2_s = ctx.enter_context(sb("dw2_s", [64, C], BF16))
        tw1 = cast_load(w1_s[:].rearrange("p (kc m) -> p kc m", m=160), w1.rearrange("(kc p) m -> p kc m", p=128),
                        deps=t_ln4, stream="p4w1")
        tw2 = cast_load(w2_s[:].rearrange("p (f c) -> p f c", c=C), w2.transpose([1, 0, 2]), deps=t_ln4, stream="p4w2")
        tdw1 = cast_load(dw1_s[:].rearrange("p (kc m) -> p kc m", m=64), dw1.rearrange("(kc p) m -> p kc m", p=128),
                         deps=t_ln4, stream="p4w3")
        tdw2 = cast_load(dw2_s[:], dw2, deps=t_ln4, stream="p4w4")

        def xn_load(buf, t0, deps, stream):
            if t0 == 0:
                return P.dma("sync", buf[:].rearrange("p (kc t) -> p kc t", t=513)[:, :, 1:513],
                             xnh[:, 0:512].rearrange("(kc p) t -> p kc t", p=128),
                             deps=deps, stream=stream)
            return P.dma("sync", buf[:].rearrange("p (kc t) -> p kc t", t=513),
                         xnh[:, t0 - 1:t0 + 512].rearrange("(kc p) t -> p kc t", p=128),
                         deps=deps, stream=stream)

        with contextlib.ExitStack() as c2:
            xtl = [c2.enter_context(sb(f"s3ax{i}", [128, NCH * 513], BF16)) for i in range(2)]
            xxb = c2.enter_context(sb("s3axx", [128, 512], BF16))
            xxx = c2.enter_context(sb("s3am", [128, NCH * 512], BF16))
            p_l = [c2.enter_context(ps(f"s3apl{i}", [32, 512], F32)) for i in range(2)]
            t_pl_free = [None, None]
            tms0 = P.emit("vector", lambda e: e.memset(
                xtl[0][:].rearrange("p (kc t) -> p kc t", t=513)[:, :, 0:1], 0.0), deps=t_ln4)
            tms1 = P.emit("vector", lambda e: e.memset(
                xtl[1][:].rearrange("p (kc t) -> p kc t", t=513)[:, :, 0:1], 0.0))
            t_x_r = [[tms0], [tms1]]
            t_xxx_pe = None
            for tb in range(NTB):
                t0 = tb * 512
                b = tb % 2
                tld = xn_load(xtl[b], t0, later(*t_x_r[b], *(t_ln4 if tb < 2 else [])), "s3ax" + str(b))
                txx = None
                for kc in range(NCH):
                    ta = P.emit("vector", lambda e, k=kc, bb=b: e.tensor_sub(
                        xxb[:], xtl[bb][:, k * 513: k * 513 + 512], xtl[bb][:, k * 513 + 1: k * 513 + 513]),
                        deps=[tld])
                    if t0 == 0:
                        ta = P.emit("vector", lambda e, k=kc, bb=b: e.tensor_scalar(
                            xxb[:, 0:1], xtl[bb][:, k * 513 + 1: k * 513 + 2], -1.0, None, op0=ALU.mult))
                    txx = P.emit("vector", lambda e, k=kc, bb=b: e.scalar_tensor_tensor(
                        xxx[:, k * 512:(k + 1) * 512], xxb[:], maa_s[:, 0 * NCH + k: 0 * NCH + k + 1],
                        xtl[bb][:, k * 513 + 1: k * 513 + 513], op0=ALU.mult, op1=ALU.add),
                        deps=later(t_xxx_pe if kc == 0 else None))
                for f in range(5):
                    bl = f % 2
                    tl = None
                    for kc in range(NCH):
                        tl = P.emit("tensor", lambda e, k=kc, ff=f, b=bl: e.matmul(
                            p_l[b][:], w1_s[:, k * 160 + ff * 32: k * 160 + ff * 32 + 32],
                            xxx[:, k * 512:(k + 1) * 512],
                            start=(k == 0), stop=(k == NCH - 1)),
                            deps=later(txx, tw1, t_pl_free[bl]) if kc == 0 else ())
                    tth = P.emit("scalar", lambda e, ff=f, tt0=t0, b=bl: e.activation(
                        lora1[:, ff * T + tt0: ff * T + tt0 + 512], p_l[b][:], AF.Tanh), deps=[tl])
                    t_pl_free[bl] = tth
                t_xxx_pe = tl
                t_x_r[b] = [tl]
        t_s3a = P.barrier()

        def emit_proj(proj_f, maa_row, wsrc, mode, dst, tagp, prev):
            with contextlib.ExitStack() as c2:
                xtl = [c2.enter_context(sb(f"{tagp}x{i}", [128, NCH * 513], BF16)) for i in range(2)]
                mixb = c2.enter_context(sb(f"{tagp}mx", [128, NCH * 512], BF16))
                xxb2 = c2.enter_context(sb(f"{tagp}xx", [128, 512], BF16))
                coef = c2.enter_context(sb(f"{tagp}cf", [128, 512], F32))
                ev = [c2.enter_context(sb(f"{tagp}ev{i}", [128, 512], BF16)) for i in range(2)]
                wt = None
                if mode == "fm":
                    wt = [c2.enter_context(sb(f"{tagp}w{i}", [128, NCH * 128], BF16)) for i in range(2)]
                elif mode == "tok":
                    wt = [c2.enter_context(sb(f"{tagp}w{i}", [128, NCH * 512], BF16)) for i in range(2)]
                ev32 = c2.enter_context(sb(f"{tagp}e32", [128, 512], F32)) if mode == "e" else None
                tnh = c2.enter_context(sb(f"{tagp}th", [64, 512], BF16)) if mode == "e" else None
                p_m = [c2.enter_context(ps(f"{tagp}pm{i}", [128, 512], F32)) for i in range(2)]
                p_p = [c2.enter_context(ps(f"{tagp}pp{i}", [128, 512], F32)) for i in range(2)]
                p_w1 = c2.enter_context(ps(f"{tagp}pw", [64, 512], F32)) if mode == "e" else None
                t_pm_free = [None, None]
                t_mix_pe = None
                t_pp_free = [None, None]
                t_ev_r = [None, None]
                t_e32_r = None
                t_coef_r = None
                t_w1_free = None
                tms0 = P.emit("vector", lambda e: e.memset(
                    xtl[0][:].rearrange("p (kc t) -> p kc t", t=513)[:, :, 0:1], 0.0), deps=prev)
                tms1 = P.emit("vector", lambda e: e.memset(
                    xtl[1][:].rearrange("p (kc t) -> p kc t", t=513)[:, :, 0:1], 0.0))
                t_x_r = [[tms0], [tms1]]
                t_w_r = [None, None]
                tlast = None
                for tb in range(NTB):
                    t0 = tb * 512
                    b = tb % 2
                    tldx = xn_load(xtl[b], t0, later(*t_x_r[b], *(prev if tb < 2 else [])), "pjx" + str(b))
                    tmx = None
                    for kc in range(NCH):
                        bm = kc % 2
                        tmk = P.emit("tensor", lambda e, f=proj_f, k=kc, tt0=t0, b=bm: e.matmul(
                            p_m[b][:, 0:512], w2_s[:, f * C + k * 128: f * C + (k + 1) * 128],
                            lora1[:, f * T + tt0: f * T + tt0 + 512], start=True, stop=True),
                            deps=later(tw2, t_pm_free[bm], *t_s3a))
                        tcf = P.emit("vector", lambda e, mr=maa_row, k=kc, b=bm: e.tensor_scalar(
                            coef[:], p_m[b][:, 0:512], maa_s[:, mr * NCH + k: mr * NCH + k + 1], None, op0=ALU.add),
                            deps=later(tmk, t_coef_r))
                        t_pm_free[bm] = tcf
                        ta = P.emit("vector", lambda e, k=kc, bb=b: e.tensor_sub(
                            xxb2[:], xtl[bb][:, k * 513: k * 513 + 512], xtl[bb][:, k * 513 + 1: k * 513 + 513]),
                            deps=[tldx])
                        if t0 == 0:
                            ta = P.emit("vector", lambda e, k=kc, bb=b: e.tensor_scalar(
                                xxb2[:, 0:1], xtl[bb][:, k * 513 + 1: k * 513 + 2], -1.0, None, op0=ALU.mult))
                        P.emit("vector", lambda e, k=kc: e.tensor_tensor(
                            mixb[:, k * 512:(k + 1) * 512], xxb2[:], coef[:], op=ALU.mult),
                            deps=later(t_mix_pe if kc == 0 else None))
                        tmx = P.emit("vector", lambda e, k=kc, bb=b: e.tensor_tensor(
                            mixb[:, k * 512:(k + 1) * 512], mixb[:, k * 512:(k + 1) * 512],
                            xtl[bb][:, k * 513 + 1: k * 513 + 513], op=ALU.add))
                    t_x_r[b] = [tmx]
                    if mode == "fm":
                        for nt2 in range(NCH):
                            bp = nt2 % 2
                            twp = P.dma("sync", wt[bp][:].rearrange("p (kc m) -> p kc m", m=128),
                                        wsrc[:, nt2 * 128:(nt2 + 1) * 128].rearrange("(kc p) m -> p kc m", p=128),
                                        deps=later(t_w_r[bp], tBW[4],
                                                   *(prev if tb == 0 and nt2 < 2 else [])),
                                        stream="pjw" + str(bp))
                            tp = None
                            for kc in range(NCH):
                                tp = P.emit("tensor", lambda e, k=kc, p=bp: e.matmul(
                                    p_p[p][:], wt[p][:, k * 128:(k + 1) * 128],
                                    mixb[:, k * 512:(k + 1) * 512], start=(k == 0), stop=(k == NCH - 1)),
                                    deps=later(tmx, t_pp_free[bp], twp) if kc == 0 else ())
                            t_w_r[bp] = tp
                            tev = P.emit("vector", lambda e, p=bp: e.tensor_copy(ev[p][:], p_p[p][:]),
                                         deps=later(tp, t_ev_r[bp]))
                            t_pp_free[bp] = tev
                            ck0 = t0 // 128
                            dsta = dst[ck0:ck0 + 4, :, :]
                            ts1 = P.dma("scalar",
                                        dsta[:, :, (2 * nt2) * 128:(2 * nt2) * 128 + 128].transpose([1, 0, 2]),
                                        ev[bp][0:64, :].rearrange("p (cc t) -> p cc t", t=128),
                                        deps=[tev], stream="pjs" + str(bp))
                            ts2 = P.dma("scalar",
                                        dsta[:, :, (2 * nt2 + 1) * 128:(2 * nt2 + 1) * 128 + 128].transpose([1, 0, 2]),
                                        ev[bp][64:128, :].rearrange("p (cc t) -> p cc t", t=128),
                                        deps=[tev], stream="pjs" + str(bp))
                            t_ev_r[bp] = ts2
                            tlast = ts2
                            t_mix_pe = tp
                    elif mode == "tok":
                        for ng in range(4):
                            bp = ng % 2
                            twp = P.dma("sync", wt[bp][:].rearrange("p (kc m) -> p kc m", m=512),
                                        wsrc[:, ng * 512:(ng + 1) * 512].rearrange("(kc p) m -> p kc m", p=128),
                                        deps=later(t_w_r[bp], tBW[4],
                                                   *(prev if tb == 0 and ng < 2 else [])),
                                        stream="pjw" + str(bp))
                            tp = None
                            for tt4 in range(4):
                                for kc in range(NCH):
                                    tp = P.emit("tensor", lambda e, k=kc, s=tt4, p=bp: e.matmul(
                                        p_p[p][:], mixb[:, k * 512 + s * 128: k * 512 + s * 128 + 128],
                                        wt[p][:, k * 512:(k + 1) * 512],
                                        start=(k == 0), stop=(k == NCH - 1)),
                                        deps=later(tmx, t_pp_free[bp], twp) if kc == 0 else ())
                                if dst is gg_h:
                                    tev = P.emit("scalar", lambda e, p=bp: e.activation(ev[p][:], p_p[p][:], AF.Silu),
                                                 deps=later(tp, t_ev_r[bp]))
                                else:
                                    tev = P.emit("vector", lambda e, p=bp: e.tensor_copy(ev[p][:], p_p[p][:]),
                                                 deps=later(tp, t_ev_r[bp]))
                                t_pp_free[bp] = tev
                                ts1 = P.dma("scalar", dst[t0 + tt4 * 128: t0 + tt4 * 128 + 128, ng * 512:(ng + 1) * 512],
                                            ev[bp][:], deps=[tev], stream="pjs" + str(bp))
                                t_ev_r[bp] = ts1
                                tlast = ts1
                                t_mix_pe = tp
                            t_w_r[bp] = tp
                    else:
                        tq1 = None
                        for kc in range(NCH):
                            tq1 = P.emit("tensor", lambda e, k=kc: e.matmul(
                                p_w1[:], dw1_s[:, k * 64:(k + 1) * 64], mixb[:, k * 512:(k + 1) * 512],
                                start=(k == 0), stop=(k == NCH - 1)),
                                deps=later(tmx, tdw1, t_w1_free) if kc == 0 else ())
                        tth2 = P.emit("scalar", lambda e: e.activation(tnh[:], p_w1[:], AF.Tanh), deps=[tq1])
                        t_w1_free = tth2
                        t_mix_pe = tq1
                        for nt2 in range(NCH):
                            bp = nt2 % 2
                            tp = P.emit("tensor", lambda e, n=nt2, p=bp: e.matmul(
                                p_p[p][:], dw2_s[:, n * 128:(n + 1) * 128], tnh[:],
                                start=True, stop=True), deps=later(tth2, tdw2, t_pp_free[bp]))
                            tcf2 = P.emit("vector", lambda e, p=bp, n=nt2: e.tensor_scalar(
                                coef[:], p_p[p][:], maa_s[:, 6 * NCH + n: 6 * NCH + n + 1], None, op0=ALU.add),
                                deps=later(tp, t_coef_r))
                            t_pp_free[bp] = tcf2
                            tex = P.emit("scalar", lambda e: e.activation(ev32[:], coef[:], AF.Exp),
                                         deps=later(tcf2, t_e32_r))
                            t_coef_r = tex
                            ck0 = t0 // 128
                            dsta = dst[ck0:ck0 + 4, :, :]
                            ts1 = P.dma("scalar",
                                        dsta[:, :, (2 * nt2) * 128:(2 * nt2) * 128 + 128].transpose([1, 0, 2]),
                                        ev32[0:64, :].rearrange("p (cc t) -> p cc t", t=128),
                                        deps=[tex], stream="pjs0")
                            ts2 = P.dma("scalar",
                                        dsta[:, :, (2 * nt2 + 1) * 128:(2 * nt2 + 1) * 128 + 128].transpose([1, 0, 2]),
                                        ev32[64:128, :].rearrange("p (cc t) -> p cc t", t=128),
                                        deps=[tex], stream="pjs0")
                            t_e32_r = ts2
                            tlast = ts2
                return tlast

        emit_proj(0, 1, None, "e", ee_h, "pw_", t_s3a)
        pb1 = P.barrier()
        emit_proj(3, 4, brB, "fm", rr_h, "pr_", pb1)
        pb2 = P.barrier()
        emit_proj(1, 2, bkB, "fm", kk_h, "pk_", pb2)
        pb3 = P.barrier()
        emit_proj(2, 3, bvB, "tok", vb_h, "pv_", pb3)
        pb4 = P.barrier()
        emit_proj(4, 5, bgB, "tok", gg_h, "pg_", pb4)
        t_s3b = P.barrier()
    if stop_after.startswith("s3b"):
        which = {"s3b_r": rr_h, "s3b_k": kk_h, "s3b_e": ee_h, "s3b_v": vb_h, "s3b_g": gg_h,
                 "s3b_x": xnh}.get(stop_after, ee_h)
        if which in (vb_h, gg_h):
            dsrc = which.rearrange("t c -> c t") if False else which
            tdmp = P.dma("gpsimd", out.rearrange("c t -> t c") if False else out, which.rearrange("t (a b) -> (t a) b", a=1) if False else which.rearrange("t c -> (t c)").rearrange("(a b) -> a b", a=2048), deps=[(k, P.cnt[k]) for k in P.cnt], stream="dump")
        elif which is xnh:
            tdmp = P.dma("gpsimd", out, which, deps=[(k, P.cnt[k]) for k in P.cnt], stream="dump")
        else:
            tdmp = P.dma("gpsimd", out, which.rearrange("n d m -> (n d) m"), deps=[(k, P.cnt[k]) for k in P.cnt], stream="dump")
        P.emit("scalar", lambda e: e.nop(), deps=[tdmp])
        P.build(); glob.close(); return nc

    with contextlib.ExitStack() as c2:
        uT_s = c2.enter_context(sb("u_s", [WD, WH], F32))
        lnw_b = c2.enter_context(sb("lnw_b", [128, 2 * C], BF16))
        lnrow = c2.enter_context(sb("lnrow", [1, 2 * C], BF16))
        tu = P.dma("sync", uT_s[:], uT, deps=t_s3b, stream="p4u")
        tlr = cast_load(lnrow[:], lnwb.rearrange("r c -> (r c)").unsqueeze(0), deps=t_s3b, stream="p4lr")
        rr1 = c2.enter_context(sb("rr1", [WD, WH * 128], BF16))
        kk1 = c2.enter_context(sb("kk1", [WD, WH * 128], BF16))
        ee1 = c2.enter_context(sb("ee1", [WD, WH * 128], F32))
        vv1b = [c2.enter_context(sb(f"vv1{i}", [128, C], BF16)) for i in range(2)]
        g1b = [c2.enter_context(sb(f"g1{i}", [128, C], BF16)) for i in range(2)]
        scn = c2.enter_context(sb("scn", [WD, WH * 128], F32))
        sxc = c2.enter_context(sb("sxc", [WD, WH * 128], F32))
        plc = c2.enter_context(sb("plc", [WD, WH], F32))
        rt = c2.enter_context(sb("rt", [WD, WH * 128], BF16))
        kt_ = c2.enter_context(sb("kt_", [WD, WH * 128], BF16))
        td_ = c2.enter_context(sb("td_", [WD, WH * 128], BF16))
        zerw = c2.enter_context(sb("zerw", [WD, 128], F32))
        atm4 = c2.enter_context(sb("atm4", [128, 512], BF16))
        ktr = c2.enter_context(sb("ktr", [128, 2 * WD], BF16))
        ddc = c2.enter_context(sb("ddc", [128, 2], F32))
        vddb = c2.enter_context(sb("vddb", [128, 1024], F32))
        Sst = c2.enter_context(sb("Sst", [WD, WH * WD], F32))
        Sbf = c2.enter_context(sb("Sbf", [WD, WH * WD], BF16))
        uxp = c2.enter_context(sb("uxp", [WD, WH * 128], BF16))
        onesw = c2.enter_context(sb("onesw", [WD, 128], BF16))
        sqg = c2.enter_context(sb("sqg", [128, C], BF16))
        gnr = c2.enter_context(sb("gnr", [128, 4 * WH], F32))
        zt = c2.enter_context(sb("zt", [128, C], BF16))
        zf = c2.enter_context(sb("zf", [128, NCH * 512], BF16))
        wob = [c2.enter_context(sb(f"wob{i}", [128, NCH * 128], BF16)) for i in range(2)]
        xrr = [c2.enter_context(sb(f"wvxr{i}", [128, 512], F32)) for i in range(2)]
        xoo = [c2.enter_context(sb(f"wvxo{i}", [128, 512], F32)) for i in range(2)]
        otg = [c2.enter_context(sb(f"ot_g{i}", [128, C], F32)) for i in range(2)]
        p_at = [c2.enter_context(ps(f"p_at{i}", [128, 512], F32)) for i in range(2)]
        p_out = [c2.enter_context(ps(f"p_out{i}", [128, 512], F32)) for i in range(2)]
        p_mis = [c2.enter_context(ps(f"p_mis{i}", [128, 512], F32)) for i in range(2)]
        p_tra = c2.enter_context(ps("p_tra", [128, 192], BF16))
        p_trb = c2.enter_context(ps("p_trb", [128, 128], BF16))

        tz = P.emit("vector", lambda e: e.memset(zerw[:], 0.0))
        P.emit("vector", lambda e: e.memset(Sst[:], 0.0))
        tSb = P.emit("vector", lambda e: e.memset(Sbf[:], 0.0))
        tow = P.emit("vector", lambda e: e.memset(onesw[:], 1.0))
        tux = None
        for h in range(WH):
            tux = P.emit("vector", lambda e, hh=h: e.tensor_scalar(
                uxp[:, hh * 128:(hh + 1) * 128], onesw[:], uT_s[:, hh:hh + 1], None, op0=ALU.mult),
                deps=later(tu, *t_s3b) if h == 0 else ())
        tprev = None
        tlnb = None
        for seg in range(2 * C // 512):
            tm = P.emit("tensor", lambda e, s=seg: e.matmul(
                p_at[0][:], ones1[:], lnrow[:, s * 512:(s + 1) * 512], start=True, stop=True),
                deps=later(tlr, tprev, *tCONST, *t_s3b))
            tlnb = P.emit("vector", lambda e, s=seg: e.tensor_copy(lnw_b[:, s * 512:(s + 1) * 512], p_at[0][:]),
                          deps=[tm])
            tprev = tlnb

        t_ld_r = [None] * 3
        t_ld_r2 = [None, None]
        t_g_r = [None, None]
        t_ot_r = [None, None]
        t_scn_r = None
        t_at_free = [tlnb, None]
        t_atm_pe = None
        t_out_evac = [None, None]
        t_tra_free = None
        t_trb_free = None
        t_out_free = [None, None]
        t_mis_free = [None, None]
        t_rtkt_pe = None
        t_Sbf_pe = None
        tSbf = tSb
        t_plc_r = None
        t_sxc_r = None
        t_gn_act = None
        t_zt_pe = None
        t_zf_pe = None
        t_wo_free = [None, None]
        t_wob_r = [None, None]
        t_xr_r3 = [None, None]
        t_xo_r3 = [None, None]

        NCKt = T // 128

        def stage2(rec):
            nonlocal t_gn_act, t_zt_pe, t_zf_pe, t_tra_free, t_ot_r, t_g_r, tzf_done
            ck, tld_g, g1, otb, tev8l, ccc = rec
            tsq2 = P.emit("scalar", lambda e, o=otb: e.activation(sqg[:], o[:], AF.Square),
                          deps=later(tev8l, t_gn_act))
            tgs = P.emit("vector", lambda e, o=otb: e.reduce_sum(
                gnr[:, 0:WH].rearrange("p (s o) -> p s o", o=1),
                o[:].rearrange("p (s o) -> p s o", o=64), axis=AX), deps=[tsq2])
            P.emit("vector", lambda e: e.reduce_sum(
                gnr[:, WH:2 * WH].rearrange("p (s o) -> p s o", o=1),
                sqg[:].rearrange("p (s o) -> p s o", o=64), axis=AX))
            P.emit("vector", lambda e: e.tensor_scalar(
                gnr[:, 2 * WH:3 * WH], gnr[:, 0:WH], -1.0 / 64, None, op0=ALU.mult))
            P.emit("vector", lambda e: e.tensor_tensor(
                gnr[:, 3 * WH:4 * WH], gnr[:, 2 * WH:3 * WH], gnr[:, 2 * WH:3 * WH], op=ALU.mult))
            P.emit("vector", lambda e: e.scalar_tensor_tensor(
                gnr[:, WH:2 * WH], gnr[:, WH:2 * WH], 1.0 / 64, gnr[:, 3 * WH:4 * WH],
                op0=ALU.mult, op1=ALU.subtract))
            P.emit("vector", lambda e: e.tensor_scalar(
                gnr[:, WH:2 * WH], gnr[:, WH:2 * WH], GN_EPS, None, op0=ALU.add))
            tg5 = P.emit("vector", lambda e: e.reciprocal(gnr[:, 3 * WH:4 * WH], gnr[:, WH:2 * WH]))
            tg6 = P.emit("scalar", lambda e: e.activation(gnr[:, WH:2 * WH], gnr[:, 3 * WH:4 * WH], AF.Sqrt),
                         deps=[tg5])
            tzt = None
            for h in range(WH):
                tzt = P.emit("vector", lambda e, hh=h, o=otb: e.tensor_scalar(
                    zt[:, hh * 64:(hh + 1) * 64], o[:, hh * 64:(hh + 1) * 64],
                    gnr[:, 2 * WH + hh:2 * WH + hh + 1], gnr[:, WH + hh:WH + hh + 1],
                    op0=ALU.add, op1=ALU.mult),
                    deps=later(tg6, t_zt_pe) if h == 0 else ())
            t_gn_act = tzt
            t_ot_r[ck % 2] = tzt
            P.emit("vector", lambda e: e.tensor_tensor(zt[:], zt[:], lnw_b[:, 0:C], op=ALU.mult),
                   deps=[tlnb])
            P.emit("vector", lambda e: e.tensor_tensor(zt[:], zt[:], lnw_b[:, C:2 * C], op=ALU.add))
            tz3 = P.emit("vector", lambda e, gg=g1: e.tensor_tensor(zt[:], zt[:], gg[:], op=ALU.mult), deps=[tld_g])
            t_g_r[ck % 2] = tz3
            for kc in range(NCH):
                ttr2 = P.emit("tensor", lambda e, k=kc: e.transpose(
                    p_tra[:, 64:192], zt[:, k * 128:(k + 1) * 128], ident[:]),
                    deps=later(tz3, t_tra_free, t_zf_pe if (ccc, kc) == (0, 0) else None))
                tzf_done = P.emit("vector", lambda e, k=kc, oo=ccc: e.tensor_copy(
                    zf[:, k * 512 + oo * 128: k * 512 + oo * 128 + 128], p_tra[:, 64:192]),
                    deps=[ttr2])
                t_tra_free = tzf_done
                if kc == NCH - 1:
                    t_zt_pe = ttr2

        def wo_group(cg):
            nonlocal t_wob_r, t_xr_r3, t_wo_free, t_xo_r3, t_at_free, t_zf_pe
            t0g = cg * 512
            tmo = None
            for nt in range(NCH):
                bwo = nt % 2
                pw = p_at[0] if bwo == 0 else p_at[1]
                twob_ = P.dma("sync", wob[bwo][:].rearrange("p (kc m) -> p kc m", m=128),
                              boB[:, nt * 128:(nt + 1) * 128].rearrange("(kc p) m -> p kc m", p=128),
                              deps=later(t_wob_r[bwo], tBW[4], *(t_s3b if cg == 0 and nt < 2 else [])),
                              stream="p4wo" + str(bwo))
                txl = P.dma("sync", xrr[bwo][:], xbb[nt * 128:(nt + 1) * 128, t0g:t0g + 512],
                            deps=later(t_xr_r3[bwo], *(t_s3b if cg == 0 and nt < 2 else [])),
                            stream="wvx" + str(bwo))
                for kc in range(NCH):
                    tmo = P.emit("tensor", lambda e, k=kc, b=bwo, pb=pw: e.matmul(
                        pb[:], wob[b][:, k * 128:(k + 1) * 128],
                        zf[:, k * 512:(k + 1) * 512], start=(k == 0), stop=(k == NCH - 1)),
                        deps=later(tzf_done, twob_, t_wo_free[bwo],
                                   t_at_free[bwo]) if kc == 0 else ())
                t_wob_r[bwo] = tmo
                tad = P.emit("vector", lambda e, b=bwo, pb=pw: e.tensor_tensor(
                    xoo[b][:], pb[:], xrr[b][:], op=ALU.add),
                    deps=later(tmo, txl, t_xo_r3[bwo]))
                t_wo_free[bwo] = tad
                t_xr_r3[bwo] = tad
                t_at_free[bwo] = tad
                tso = P.dma("scalar", xbb[nt * 128:(nt + 1) * 128, t0g:t0g + 512], xoo[bwo][:],
                            deps=[tad], stream="wvs" + str(bwo))
                t_xo_r3[bwo] = tso
            t_zf_pe = tmo

        pend = None
        tzf_done = None
        for ck in range(NCKt):
            cc = ck % 4
            tld_r = P.dma("sync", rr1[:], rr_h[ck], deps=later(t_ld_r[0], *t_s3b), stream="wvr")
            tld_k = P.dma("sync", kk1[:], kk_h[ck], deps=later(t_ld_r[1], *t_s3b), stream="wvk")
            tld_e = P.dma("sync", ee1[:], ee_h[ck], deps=later(t_ld_r[2], *t_s3b), stream="wve")
            vv1 = vv1b[ck % 2]
            g1 = g1b[ck % 2]
            otb = otg[ck % 2]
            tld_v = P.dma("gpsimd", vv1[:], vb_h[ck * 128:(ck + 1) * 128, :],
                          deps=later(t_ld_r2[ck % 2], *t_s3b), stream="wvv" + str(ck % 2))
            tld_g = P.dma("gpsimd", g1[:], gg_h[ck * 128:(ck + 1) * 128, :],
                          deps=later(t_g_r[ck % 2], *t_s3b), stream="wvg" + str(ck % 2))
            tsc = None
            for h in range(WH):
                tsc = P.emit("vector", lambda e, hh=h: e.tensor_tensor_scan(
                    scn[:, hh * 128:(hh + 1) * 128], ee1[:, hh * 128:(hh + 1) * 128],
                    zerw[:], 0.0, op0=ALU.add, op1=ALU.bypass),
                    deps=later(tld_e, tz, t_scn_r) if h == 0 else ())
            tsx = P.emit("vector", lambda e: e.tensor_tensor(sxc[:], scn[:], ee1[:], op=ALU.subtract),
                         deps=later(tsc, t_sxc_r))
            tPL = P.emit("scalar", lambda e: e.activation(
                plc[:], scn[:].rearrange("p (h t) -> p h t", t=128)[:, :, 127], AF.Exp, scale=cst[0:WD, 2:3]),
                deps=later(tsc, t_plc_r))
            tQ_ = P.emit("scalar", lambda e: e.activation(scn[:], scn[:], AF.Exp), deps=[tPL, tsx])
            tP_ = P.emit("scalar", lambda e: e.activation(sxc[:], sxc[:], AF.Exp, scale=cst[0:WD, 2:3]), deps=[tsx])
            t_scn_r = tQ_
            t_sxc_r = tP_
            ttd1 = P.emit("vector", lambda e: e.tensor_tensor(td_[:], rr1[:], kk1[:], op=ALU.mult),
                          deps=later(tld_r, tld_k, tux, t_rtkt_pe))
            ttd2 = P.emit("vector", lambda e: e.tensor_tensor(td_[:], td_[:], uxp[:], op=ALU.mult))
            trt = P.emit("vector", lambda e: e.tensor_tensor(rt[:], rr1[:], sxc[:], op=ALU.mult),
                         deps=later(tP_, tld_r))
            tkt = P.emit("vector", lambda e: e.tensor_tensor(kt_[:], kk1[:], scn[:], op=ALU.mult),
                         deps=later(tQ_, tld_k))
            tvda = None
            tev8 = None
            for hg in range(WH // 4):
                ba = hg % 2
                tat = None
                for hi in range(4):
                    h = hg * 4 + hi
                    tat = P.emit("tensor", lambda e, hh=h, b=ba, r4=hi * 128: e.matmul(
                        p_at[b][:, r4:r4 + 128], kt_[:, hh * 128:(hh + 1) * 128], rt[:, hh * 128:(hh + 1) * 128],
                        start=True, stop=True),
                        deps=later(tkt, trt, t_at_free[ba]) if hi == 0 else ())
                tatm = P.emit("vector", lambda e, b=ba: e.tensor_tensor(atm4[:], p_at[b][:], mst4_t[:], op=ALU.mult),
                              deps=later(tat, t_atm_pe))
                t_at_free[ba] = tatm
                for hi in range(4):
                    h = hg * 4 + hi
                    bo = (h // 8) % 2
                    ro = (h % 8) * 64
                    if h % 2 == 0:
                        for h2 in range(2):
                            ttdt = P.emit("tensor", lambda e, hh=h + h2, r2=h2 * 64: e.transpose(
                                p_tra[:, r2:r2 + 64], td_[:, hh * 128:(hh + 1) * 128], ident[0:64, 0:64]),
                                deps=later(ttd2, t_tra_free) if h2 == 0 else ())
                        tred = P.emit("vector", lambda e: e.reduce_sum(
                            ddc[:].rearrange("p (s o) -> p s o", o=1),
                            p_tra[:, 0:128].rearrange("p (s o) -> p s o", o=64), axis=AX),
                            deps=[ttdt])
                        t_tra_free = tred
                    tvd = P.emit("vector", lambda e, hh=h, vv=vv1: e.tensor_scalar(
                        vddb[:, ((hh // 8) % 2) * 512 + (hh % 8) * 64: ((hh // 8) % 2) * 512 + (hh % 8) * 64 + 64],
                        vv[:, hh * 64:(hh + 1) * 64], ddc[:, hh % 2:hh % 2 + 1], None, op0=ALU.mult),
                        deps=[tld_v])
                    to1 = P.emit("tensor", lambda e, b=bo, r=ro, hh=h, vv=vv1: e.matmul(
                        p_out[b][:, r:r + 64], atm4[:, (hh % 4) * 128:(hh % 4) * 128 + 128],
                        vv[:, hh * 64:(hh + 1) * 64],
                        start=True, stop=False),
                        deps=later(tatm, tld_v, t_out_evac[bo] if h % 8 == 0 else None))
                    t_atm_pe = to1 if hi == 3 else t_atm_pe
                    to2 = P.emit("tensor", lambda e, b=bo, r=ro, hh=h: e.matmul(
                        p_out[b][:, r:r + 64], rt[:, hh * 128:(hh + 1) * 128],
                        Sbf[:, hh * 64:(hh + 1) * 64], start=False, stop=True), deps=[tSbf])
                    if h % 8 == 7:
                        seg = h // 8
                        tev8 = P.emit("vector", lambda e, b=(h // 8) % 2, s=seg, o=otb: e.tensor_tensor(
                            o[:, s * 512:(s + 1) * 512], p_out[b][:],
                            vddb[:, (s % 2) * 512:(s % 2) * 512 + 512], op=ALU.add),
                            deps=later(to2, tvd, t_ot_r[ck % 2] if h == 7 else None))
                        t_out_evac[(h // 8) % 2] = tev8
                    if h % 2 == 0:
                        for h2 in range(2):
                            tktt = P.emit("tensor", lambda e, hh=h + h2, r2=h2 * 64: e.transpose(
                                p_trb[:, r2:r2 + 64], kt_[:, hh * 128:(hh + 1) * 128], ident[0:64, 0:64]),
                                deps=later(tkt, t_trb_free) if h2 == 0 else ())
                        tktr = P.emit("vector", lambda e: e.tensor_copy(ktr[:], p_trb[:, 0:128]), deps=[tktt])
                        t_trb_free = tktr
                    tkv = P.emit("tensor", lambda e, hh=h, b=ba, vv=vv1: e.matmul(
                        p_mis[b][0:64, 0:64], ktr[:, (hh % 2) * 64:(hh % 2) * 64 + 64],
                        vv[:, hh * 64:(hh + 1) * 64],
                        start=True, stop=True), deps=later(tktr, tld_v, t_mis_free[ba]))
                    ts1_ = P.emit("vector", lambda e, hh=h, b=ba: e.tensor_tensor(
                        Sst[:, hh * 64:(hh + 1) * 64], Sst[:, hh * 64:(hh + 1) * 64], p_mis[b][0:64, 0:64],
                        op=ALU.add), deps=[tkv])
                    t_mis_free[ba] = ts1_
                    ts2_ = P.emit("vector", lambda e, hh=h: e.tensor_scalar(
                        Sst[:, hh * 64:(hh + 1) * 64], Sst[:, hh * 64:(hh + 1) * 64], plc[:, hh:hh + 1], None,
                        op0=ALU.mult), deps=[tPL])
                    t_plc_r = ts2_
                    t_rtkt_pe = tkv
                    t_Sbf_pe = to2
            tSbf = P.emit("vector", lambda e: e.tensor_copy(Sbf[:], Sst[:]), deps=later(t_Sbf_pe))
            t_ld_r = [trt, tkt, tsx]
            t_ld_r2[ck % 2] = ts1_
            rec = (ck, tld_g, g1, otb, tev8, cc)
            if pend is not None:
                stage2(pend)
                if pend[0] % 4 == 3:
                    wo_group(pend[0] // 4)
            pend = rec
        stage2(pend)
        wo_group(pend[0] // 4)
    t_p4_done = P.barrier()
    if stop_after == "p4":
        tdmp = P.dma("gpsimd", out, xbb, deps=[(k, P.cnt[k]) for k in P.cnt], stream="dump")
        P.emit("scalar", lambda e: e.nop(), deps=[tdmp])
        P.build(); glob.close(); return nc

    ffn_phase(xbb, out, "f2", t_p4_done)

    fin = [(k, P.cnt[k]) for k in P.cnt]
    P.emit("scalar", lambda e: e.nop(), deps=fin)

    P.build()
    glob.close()
    return nc


def _prep_inputs(ins):
    f = np.float32

    def T_(a):
        return np.ascontiguousarray(np.asarray(a, dtype=f).T)

    def u8(a):
        v = np.ascontiguousarray(a).view(np.uint8)[:, 0::4]
        return np.ascontiguousarray(v)

    b = np.arange(8192)
    perm = np.empty(MLP, dtype=np.int64)
    perm[(b // 128) * 256 + (b % 128)] = 2 * b
    perm[(b // 128) * 256 + 128 + (b % 128)] = 2 * b + 1

    maa = np.stack([np.asarray(ins["L0_attn_b_time_maa_" + k], f).reshape(C) for k in
                    ["x", "w", "k", "v", "r", "g"]] + [np.asarray(ins["L0_attn_b_time_decay"], f).reshape(C)])
    maa_p = np.ascontiguousarray(maa.reshape(7, NCH, 128).transpose(2, 0, 1).reshape(128, 7 * NCH))
    lnwb = np.stack([np.asarray(ins["L0_attn_b_ln_w"], f).reshape(C), np.asarray(ins["L0_attn_b_ln_b"], f).reshape(C)])

    common = {
        "wqT": T_(ins["L0_attn_a_Wq"]), "wkT": T_(ins["L0_attn_a_Wk"]),
        "wvT": T_(ins["L0_attn_a_Wv"]), "woT": T_(ins["L0_attn_a_Wo"]),
        "brT": T_(ins["L0_attn_b_W_r"]), "bkT": T_(ins["L0_attn_b_W_k"]),
        "bvT": T_(ins["L0_attn_b_W_v"]), "bgT": T_(ins["L0_attn_b_W_g"]),
        "boT": T_(ins["L0_attn_b_W_o"]),
        "maa": maa_p, "lnwb": np.ascontiguousarray(lnwb),
        "w1": np.ascontiguousarray(np.asarray(ins["L0_attn_b_time_maa_w1"], f)),
        "w2": np.ascontiguousarray(np.asarray(ins["L0_attn_b_time_maa_w2"], f)),
        "dw1": np.ascontiguousarray(np.asarray(ins["L0_attn_b_time_decay_w1"], f)),
        "dw2": np.ascontiguousarray(np.asarray(ins["L0_attn_b_time_decay_w2"], f)),
        "uT": T_(ins["L0_attn_b_time_faaaa"]),
        "gpk": np.ascontiguousarray(u8(ins["L0_ffn_gate_W_packed"]).T[:, perm]),
        "gsc": np.ascontiguousarray(np.asarray(ins["L0_ffn_gate_scales"], f).T[:, perm]),
        "upk": np.ascontiguousarray(u8(ins["L0_ffn_up_W_packed"]).T[:, perm]),
        "usc": np.ascontiguousarray(np.asarray(ins["L0_ffn_up_scales"], f).T[:, perm]),
        "dpk": np.ascontiguousarray(u8(ins["L0_ffn_down_W_packed"]).T),
        "dsc": np.ascontiguousarray(np.asarray(ins["L0_ffn_down_scales"], f).T),
    }
    maps = []
    for bi in range(ins["L0_attn_a_x"].shape[0]):
        m = dict(common)
        m["x"] = np.ascontiguousarray(np.asarray(ins["L0_attn_a_x"][bi], f).T)
        maps.append(m)
    return maps


def kernel(**ins):
    maps = _prep_inputs(ins)
    nc = build_program(T=4096)
    res = run_bass_kernel_spmd(nc, maps, core_ids=list(range(len(maps))))
    return np.stack([np.asarray(r["out"], dtype=np.float32).T for r in res.results])
